# version 16
# baseline (speedup 1.0000x reference)
import numpy as np
import ml_dtypes
import concourse.bass as bass
import concourse.mybir as mybir
from concourse.bass_utils import run_bass_kernel_spmd
from contextlib import ExitStack

F32 = mybir.dt.float32
BF16 = mybir.dt.bfloat16
AF = mybir.ActivationFunctionType
ALU = mybir.AluOpType
AX = mybir.AxisListType

P = 128
D = 1024
H = 4096
E = 8
NH = 16
DH = 64
KC = D // P
HC = H // P
Tg = 1024
TC = Tg // P
NTB = 2
CPT = 4
NG = 4
NHALF = 2
HCH = HC // NHALF
SLAB_H = 512
N_SLAB = (HCH * P) // SLAB_H
HC_PER_SLAB = SLAB_H // P
NPREF = 32
LN_EPS_IN = 1e-3
LN_EPS = 1e-5
DEBUG_STAGE = None


class Ctr:
    def __init__(self, handle):
        self.h = handle
        self.n = 0
    def inc(self, n=1):
        self.n += n
        return self.n


class Prog:
    ENGS = ("sync", "gpsimd", "tensor", "vector", "scalar")
    def __init__(self):
        self.ops = {k: [] for k in self.ENGS}
    def emit(self, eng, fn):
        self.ops[eng].append(fn)


def host_consts():
    g = 1.0 - 2.0 ** (-5.0 - np.arange(NH, dtype=np.float64))
    t = np.arange(P, dtype=np.float64)
    diff = t[None, :] - t[:, None]
    maskD = np.zeros((P, NH * P), np.float64)
    for h in range(NH):
        maskD[:, h * P:(h + 1) * P] = np.where(diff >= 0, g[h] ** np.maximum(diff, 0) / 8.0, 0.0)
    decQ = np.zeros((P, KC * P), np.float64)
    for i in range(KC):
        for hf in range(2):
            hh = 2 * i + hf
            decQ[hf * DH:(hf + 1) * DH, i * P:(i + 1) * P] = (g[hh] ** (t + 1.0) / 8.0)[None, :]
    decK = np.zeros((P, KC * P), np.float64)
    for i in range(KC):
        for hf in range(2):
            hh = 2 * i + hf
            decK[:, i * P + hf * DH: i * P + (hf + 1) * DH] = (g[hh] ** (127.0 - t))[:, None]
    gC = np.zeros((P, 512), np.float64)
    for h in range(NH):
        gC[DH * (h % 2):DH * (h % 2 + 1), (h // 2) * DH:(h // 2 + 1) * DH] = g[h] ** 128.0
    return {
        "maskD": np.ascontiguousarray(maskD.astype(ml_dtypes.bfloat16)),
        "decQ": np.ascontiguousarray(decQ.astype(ml_dtypes.bfloat16)),
        "decK": np.ascontiguousarray(decK.astype(ml_dtypes.bfloat16)),
        "gC": np.ascontiguousarray(gC.astype(np.float32)),
        "idn": np.eye(P, dtype=np.float32),
    }


def build_program():
    nc = bass.Bass("TRN2", target_bir_lowering=False)
    dt = lambda name, shape, dtype=F32, kind="ExternalInput": nc.dram_tensor(name, shape, dtype, kind=kind).ap()

    x_d = dt("x", [NG * Tg, D])
    xp_d = dt("xp", [NPREF * P, D])
    wq_d = dt("Wq", [D, D]); wk_d = dt("Wk", [D, D]); wv_d = dt("Wv", [D, D])
    wg_d = dt("Wg", [D, D]); wo_d = dt("Wo", [D, D])
    bgb_d = dt("bg_b", [P, D], BF16)
    bob_d = dt("bo_b", [P, D], BF16)
    gam_d = dt("gam_b", [P, D], BF16)
    bet_d = dt("bet_b", [P, D], BF16)
    w1_d = dt("W1", [E, D, H]); b1_d = dt("b1", [E, H])
    w2_d = dt("W2", [E, H, D]); b2_d = dt("b2", [E, D])
    wr_d = dt("Wr", [D, E]); brb_d = dt("br_b", [P, E])
    maskD_d = dt("maskD", [P, NH * P], BF16)
    decQ_d = dt("decQ", [P, KC * P], BF16)
    decK_d = dt("decK", [P, KC * P], BF16)
    gC_d = dt("gC", [P, 512])
    idn_d = dt("idn", [P, P])
    y_d = dt("y", [NG * Tg, D], F32, kind="ExternalOutput")
    xbf_scr = dt("xbf_scr", [NG * Tg, D], BF16, kind="Internal")
    xpbf_scr = dt("xpbf_scr", [NPREF * P, D], BF16, kind="Internal")
    wA_scr = dt("wA_scr", [5 * P, KC * D], BF16, kind="Internal")
    w1_scr = dt("w1_scr", [E * P, KC * H], BF16, kind="Internal")
    w2_scr = dt("w2_scr", [E * P, HC * D], BF16, kind="Internal")

    ctx = ExitStack()
    sb = lambda n, s, d=F32: ctx.enter_context(nc.sbuf_tensor(n, s, d))
    ps = lambda n, s, d=F32: ctx.enter_context(nc.psum_tensor(n, s, d))

    acc = sb("acc", [P, TC * D])
    y1fm = sb("y1fm", [P, KC * Tg], BF16)
    regC = sb("regC", [P, HC * D], BF16)
    w2buf = regC
    WQ_OFF, WK_OFF, WV_OFF, WG_OFF = 0, KC * D, 2 * KC * D, 3 * KC * D
    regD = sb("regD", [P, HCH * Tg], BF16)
    hbuf = regD
    WO_OFF = 0
    QF_OFF = KC * D
    KF_OFF = KC * D + KC * 512
    regE = sb("regE", [P, 2 * KC * SLAB_H], BF16)
    w1buf = [regE[:, i * KC * SLAB_H:(i + 1) * KC * SLAB_H] for i in range(2)]
    XF_OFF = 0
    VT_OFF = KC * 512
    GT_OFF = KC * 512 + D
    KT_OFF = KC * 512 + 2 * D
    QD_OFF = KC * 512 + 3 * D
    regF = sb("regF", [P, 3 * D])
    xtm = [regF[:, 0:D], regF[:, D:2 * D]]
    t1r1 = regF[:, 2 * D:3 * D]
    outsb = [regF[:, 0:D], regF[:, D:2 * D]]
    b1stg = regF[0:HC, 0:E * P]
    b2stg = regF[0:E, D:2 * D]
    regG = sb("regG", [P, 3 * D], BF16)
    lnG = regG[:, 0:D]
    stbf = [regG[:, D:D + 512], regG[:, D + 512:2 * D]]
    xy1bfc = regG[:, 2 * D:3 * D]
    wfmhi = regG[0:E, D:2 * D]
    wfmlo = regG[0:E, 2 * D:3 * D]
    maskD = sb("maskDs", [P, NH * P], BF16)
    decQ = sb("decQs", [P, KC * P], BF16)
    decK = sb("decKs", [P, KC * P], BF16)
    gCt = sb("gCt", [P, 512])
    gamb = sb("gamb", [P, D], BF16)
    betb = sb("betb", [P, D], BF16)
    bgb = sb("bgb", [P, D], BF16)
    bob = sb("bob", [P, D], BF16)
    idn_f = sb("idn_f", [P, P])
    idn_b = sb("idn_b", [P, P], BF16)
    b1fm = sb("b1fm", [P, E * HC])
    b2hi = sb("b2hi", [E, D], BF16)
    b2lo = sb("b2lo", [E, D], BF16)
    wrsb = sb("wrsb", [P, KC * E], BF16)
    brb = sb("brb", [P, E])
    S = sb("S", [P, 512])
    Sbf = sb("Sbf", [P, 512], BF16)
    wsb = sb("wsb", [P, TC * E])
    mvar = sb("mvar", [P, 2])
    bnbuf = sb("bnbuf", [P, 12])
    lnv = sb("lnv", [P, 1])
    rstd = sb("rstd", [P, 1])
    nbias = sb("nbias", [P, 1])
    epsi = sb("epsi", [P, 1])
    eps5 = sb("eps5", [P, 1])
    rl = sb("rl", [P, CPT * E])
    rl2 = sb("rl2", [P, CPT * E])
    rm1 = sb("rm1", [P, CPT])
    rm2 = sb("rm2", [P, CPT])
    req = sb("req", [P, CPT * E])
    rge = sb("rge", [P, CPT * E])
    rexp = sb("rexp", [P, CPT * E])
    rsum = sb("rsum", [P, CPT])
    rrs = sb("rrs", [P, CPT])

    pA = ps("pA", [P, D])
    pB = ps("pB", [P, D])
    pC = ps("pC", [P, D], BF16)
    pEF = ps("pEF", [P, 1024])
    pE = pEF[:, 0:512]
    pF = pEF[:, 512:1024]
    PE_QUARTER = pEF[:, 0:256]
    PF_QUARTER = pEF[:, 512:768]
    pW = ps("pW", [P, 512])

    sem_names = ("q_c", "q_x0", "q_x1", "q_xc", "q_xe", "q_wa", "q_w2", "q_w1a", "q_w1b", "q_o0", "q_o1", "q_cast", "pe", "act", "dve")
    sems = {k: Ctr(ctx.enter_context(nc.semaphore(k))) for k in sem_names}
    _WA_NAMES = ("wq", "wk", "wv", "wg", "wo")
    (q_c, q_x0, q_x1, q_xc, q_xe, q_wa, q_w2, q_w1a, q_w1b, q_o0, q_o1, q_cast, pe, act, dve) = (sems[k] for k in sem_names)
    q_x = [q_x0, q_x1]
    q_w1 = [q_w1a, q_w1b]
    q_o = [q_o0, q_o1]
    prog = Prog()

    def vop(fn, n_inc=1):
        wait_n = dve.n
        def f(v):
            v.wait_ge(dve.h, wait_n)
            fn(v).then_inc(dve.h, n_inc)
        prog.emit("vector", f)
        return dve.inc(n_inc)

    def aop(fn, n_inc=1):
        wait_n = act.n
        def f(s):
            s.wait_ge(act.h, wait_n)
            fn(s).then_inc(act.h, n_inc)
        prog.emit("scalar", f)
        return act.inc(n_inc)

    def eng_waits(eng, **kw):
        items = [(sems[k].h, v) for k, v in kw.items() if v > 0]
        if not items:
            return
        def f(e):
            for h, v in items:
                e.wait_ge(h, v)
        prog.emit(eng, f)

    def sync_dma(fn, sem):
        def f(s):
            fn(s).then_inc(sem.h, 16)
        prog.emit("sync", f)
        return sem.inc(16)

    def gp_dma(fn, sem, wait_pe=0):
        def f(g):
            if wait_pe:
                g.wait_ge(pe.h, wait_pe)
            fn(g).then_inc(sem.h, 16)
        prog.emit("gpsimd", f)
        return sem.inc(16)

    st = {
        "xtm_free_dve": [0, 0],
        "xbfc_free_pe": 0,
        "pc_free": {},
        "xfm_ready": {},
        "qk_ready": {},
        "vt_ready": {},
        "gt_ready": {},
        "kt_ready": {},
        "qd_ready": {},
        "sbf_ready": {},
        "pe_pf_free": [{}, {}],
        "pA_free": {},
        "pB_free": {},
        "pW_free": {},
        "y1fm_ready": {},
        "st_free_pe_last": 0,
        "ea_ebuf_dve": 0,
        "prep_xt_pe": 0,
        "prep_q_xe": 0,
        "rl_pe": 0,
        "xbob": {},
        "xd": [0, 0],
        "gG_ready": {},
        "bG_ready": {},
        "lnG_ready": {},
        "w1_free_pe": [0, 0],
        "w2_free_pe": 0,
        "wa": {},
        "wfm_reader_pe": 0,
        "ph_free": [{}, {}, {}],
        "ffn_last_pe": 0,
        "outsb_dma": [0, 0],
        "out_dma": [],
    }
    po_free_dve = [0, 0]

    C_LOADS = sync_dma(lambda s: s.dma_start(out=idn_f[:], in_=idn_d), q_c)
    for dst, src in ((maskD, maskD_d), (decQ, decQ_d), (decK, decK_d), (gCt, gC_d),
                     (gamb, gam_d), (betb, bet_d), (bgb, bgb_d), (bob, bob_d), (brb, brb_d)):
        C_LOADS = sync_dma(lambda s, dst=dst, src=src: s.dma_start(out=dst[:], in_=src), q_c)
    C_LOADS = sync_dma(lambda s: s.dma_start(
        out=b1stg[:].rearrange("hc (e p) -> hc e p", e=E),
        in_=b1_d.rearrange("e (hc p) -> hc e p", p=P)), q_c)
    C_LOADS = sync_dma(lambda s: s.dma_start(out=b2stg[:], in_=b2_d), q_c)

    CAST = {}
    wa_sems = {nm: Ctr(ctx.enter_context(nc.semaphore("q_" + nm))) for nm in _WA_NAMES}
    for _nm, _c in wa_sems.items():
        sems["q_" + _nm] = _c
    def cast_dma(name, out_ap, in_ap):
        prev = q_cast.n
        def f(g):
            g.wait_ge(q_cast.h, prev)
            g.dma_start(out=out_ap, in_=in_ap, max_dma_last_dim=8192).then_inc(q_cast.h, 16)
        prog.emit("gpsimd", f)
        CAST[name] = q_cast.inc(16)

    XP_PIECE = 4 * P
    ATTN_W = (("wv", wv_d, 2), ("wk", wk_d, 1), ("wg", wg_d, 3), ("wq", wq_d, 0), ("wo", wo_d, 4))
    def cast_attn_w(nm, src, slot):
        cast_dma(nm, wA_scr[slot * P:(slot + 1) * P, :].rearrange("p (kc n) -> p kc n", kc=KC),
                 src.rearrange("(kc p) n -> p kc n", p=P))
    n_xp = (NPREF * P + XP_PIECE - 1) // XP_PIECE
    _wa0 = {}
    def load_attn_w1(nm, slot):
        castc = CAST[nm]
        slot_off = {"wq": (WQ_OFF, regC), "wk": (WK_OFF, regC), "wv": (WV_OFF, regC),
                    "wg": (WG_OFF, regC), "wo": (WO_OFF, regD)}
        off, reg = slot_off[nm]
        sem = wa_sems[nm]
        def f(g, off=off, reg=reg, slot=slot, castc=castc, sem=sem):
            g.wait_ge(q_cast.h, castc)
            g.dma_start(out=reg[:, off:off + KC * D],
                        in_=wA_scr[slot * P:(slot + 1) * P, :]).then_inc(sem.h, 16)
        prog.emit("gpsimd", f)
        _wa0[nm] = (nm, sem.inc(16))
    cast_dma("xp0", xpbf_scr[0:min(XP_PIECE, NPREF * P), :], xp_d[0:min(XP_PIECE, NPREF * P), :])
    cast_attn_w(*ATTN_W[0]); load_attn_w1("wv", 2)
    cast_attn_w(*ATTN_W[1]); load_attn_w1("wk", 1)
    for i in range(1, n_xp):
        r0, r1 = i * XP_PIECE, min((i + 1) * XP_PIECE, NPREF * P)
        cast_dma(f"xp{i}", xpbf_scr[r0:r1, :], xp_d[r0:r1, :])
    cast_attn_w(*ATTN_W[2]); load_attn_w1("wg", 3)
    cast_attn_w(*ATTN_W[3]); load_attn_w1("wq", 0)
    cast_attn_w(*ATTN_W[4]); load_attn_w1("wo", 4)
    for i in range((NG * Tg + XP_PIECE - 1) // XP_PIECE):
        r0, r1 = i * XP_PIECE, min((i + 1) * XP_PIECE, NG * Tg)
        cast_dma(f"x{i}", xbf_scr[r0:r1, :], x_d[r0:r1, :])
    def cast_w1(e):
        for hh in range(2):
            cast_dma(f"w1_{e}" if hh == 1 else f"w1_{e}_a",
                     w1_scr[e * P:(e + 1) * P, :].rearrange("p (kc h) -> p kc h", kc=KC)[:, :, hh * 2048:(hh + 1) * 2048],
                     w1_d[e][:, hh * 2048:(hh + 1) * 2048].rearrange("(kc p) h -> p kc h", p=P))
    gp_dma(lambda g: g.dma_start(
        out=wrsb[:].rearrange("p (kc e) -> p kc e", kc=KC),
        in_=wr_d.rearrange("(kc p) e -> p kc e", p=P), max_dma_last_dim=32), q_wa)
    WR_READY = q_wa.n
    def cast_w2(e):
        cast_dma(f"w2_{e}", w2_scr[e * P:(e + 1) * P, :].rearrange("p (hc d) -> p hc d", hc=HC),
                 w2_d[e].rearrange("(hc p) d -> p hc d", p=P))
    cast_w1(0)
    cast_w2(0)


    def w_attn_loads(wait_pe):
        wa = {}
        slot_off = {"wq": (WQ_OFF, regC), "wk": (WK_OFF, regC), "wv": (WV_OFF, regC),
                    "wg": (WG_OFF, regC), "wo": (WO_OFF, regD)}
        for nm, src, slot in ATTN_W:
            off, reg = slot_off[nm]
            castc = CAST[nm]
            sem = wa_sems[nm]
            def f(g, nm=nm, off=off, reg=reg, slot=slot, castc=castc, wait_pe=wait_pe, sem=sem):
                g.wait_ge(q_cast.h, castc)
                if wait_pe:
                    g.wait_ge(pe.h, wait_pe)
                g.dma_start(out=reg[:, off:off + KC * D],
                            in_=wA_scr[slot * P:(slot + 1) * P, :]).then_inc(sem.h, 16)
            prog.emit("gpsimd", f)
            wa[nm] = (nm, sem.inc(16))
        st["wa"] = wa

    st["wa"] = dict(_wa0)

    eng_waits("vector", q_c=C_LOADS)
    vop(lambda v: v.memset(epsi[:], LN_EPS_IN))
    vop(lambda v: v.memset(eps5[:], LN_EPS))
    vop(lambda v: v.memset(S[:], 0.0))
    SBF0 = vop(lambda v: v.memset(Sbf[:], 0.0))
    st["sbf_ready"] = {"dve": SBF0}
    vop(lambda v: v.tensor_copy(idn_b[:], idn_f[:]))
    vop(lambda v: v.tensor_copy(b2hi[:], b2stg[:]))
    PREP = vop(lambda v: v.tensor_tensor(b2lo[:], b2stg[:], b2hi[:], op=ALU.subtract))

    def f(t, w=PREP):
        t.wait_ge(dve.h, w)
        last = None
        for e in range(E):
            last = t.transpose(pW[:, e * HC:(e + 1) * HC], b1stg[:, e * P:(e + 1) * P], idn_f[0:HC, 0:HC])
        last.then_inc(pe.h, 1)
    prog.emit("tensor", f)
    B1_PE = pe.inc()
    eng_waits("vector", pe=B1_PE)
    B1FM = vop(lambda v: v.tensor_copy(b1fm[:], pW[:, 0:E * HC]))
    st["pW_free"] = {"dve": B1FM}
    st["pc_free"] = {"dve": PREP}
    st["xtm_free_dve"] = [B1FM, B1FM]

    def dmax(*deps):
        out = {}
        for d in deps:
            if not d:
                continue
            if isinstance(d, int):
                d = {"dve": d}
            for k, v in d.items():
                out[k] = max(out.get(k, 0), v)
        return out

    def twd(t, deps):
        for k, v in deps.items():
            if v:
                t.wait_ge(sems[k].h, v)

    def ew(eng, deps):
        items = [(sems[k].h, v) for k, v in deps.items() if v > 0]
        if not items:
            return
        def f(e):
            for h, v in items:
                e.wait_ge(h, v)
        prog.emit(eng, f)

    def x_prep(xsrc_bf, castc, cloc, stage=None, extra_ld=None, slot_pe=0):
        stg = xy1bfc if stage is None else stage
        wpe = st["xbfc_free_pe"] if stage is None else 0
        eld = dict(extra_ld or {})
        qs = q_xc if stage is None else q_xe
        def fldr(s_, xsrc_bf=xsrc_bf, castc=castc, wpe=wpe, prev=qs.n, eld=eld, stg=stg, qs=qs):
            s_.wait_ge(qs.h, prev)
            s_.wait_ge(q_cast.h, castc)
            if wpe:
                s_.wait_ge(pe.h, wpe)
            for k, v in eld.items():
                s_.wait_ge(sems[k].h, v)
            s_.dma_start(out=stg[:, 0:D] if stage is not None else xy1bfc[:], in_=xsrc_bf).then_inc(qs.h, 16)
        prog.emit("sync", fldr)
        XC = qs.inc(16)
        if stage is not None:
            st["prep_q_xe"] = XC
        def f(t, w=XC, wb=dict(st["pc_free"]), stg=stg, stage=stage, qs=qs):
            t.wait_ge(qs.h, w)
            twd(t, wb)
            last = None
            src = stg[:, 0:D] if stage is not None else xy1bfc[:]
            for kc in range(KC):
                last = t.transpose(pC[:, kc * P:(kc + 1) * P], src[:, kc * P:(kc + 1) * P], idn_b[:])
            last.then_inc(pe.h, 1)
        prog.emit("tensor", f)
        XT_PE = pe.inc()
        if stage is None:
            st["xbfc_free_pe"] = XT_PE
        else:
            st["prep_xt_pe"] = XT_PE
        ew("scalar", {"pe": max(XT_PE, slot_pe)})
        ev = aop(lambda s, cloc=cloc: s.copy(
            regE[:, XF_OFF:XF_OFF + KC * 512].rearrange("p (kc t) -> p kc t", kc=KC)[:, :, cloc * P:(cloc + 1) * P],
            pC[:].rearrange("p (kc t) -> p kc t", kc=KC)))
        st["pc_free"] = {"act": ev}
        st["xfm_ready"] = {"act": ev}

    def v_proj(cloc):
        def f(t, wd=dmax(st["xfm_ready"], st["pB_free"]), wq=st["wa"]["wv"], cloc=cloc):
            twd(t, wd)
            t.wait_ge(sems["q_" + wq[0]].h, wq[1]) if isinstance(wq, tuple) else t.wait_ge(q_wa.h, wq)
            last = None
            for nh2 in range(2):
                for kc in range(KC):
                    last = t.matmul(pB[:, nh2 * 512:(nh2 + 1) * 512],
                                    regE[:, XF_OFF + kc * 512 + cloc * P: XF_OFF + kc * 512 + (cloc + 1) * P],
                                    regC[:, WV_OFF + kc * D + nh2 * 512: WV_OFF + kc * D + (nh2 + 1) * 512],
                                    start=(kc == 0), stop=(kc == KC - 1))
            last.then_inc(pe.h, 1)
        prog.emit("tensor", f)
        V_PE = pe.inc()
        ew("scalar", {"pe": V_PE})
        ev = aop(lambda s: s.copy(regE[:, VT_OFF:VT_OFF + D], pB[:]))
        st["vt_ready"] = {"act": ev}
        st["pB_free"] = {"act": ev}

    def sdelta_mm():
        def f(t, wd=dmax(st["kt_ready"], st["vt_ready"], st["pB_free"])):
            twd(t, wd)
            last = None
            for h in range(NH):
                pb_ = DH * (h % 2)
                col = (h // 2) * DH + 512 * (h % 2)
                dst = pB[pb_:pb_ + DH, col:col + DH]
                last = t.matmul(dst, regE[:, KT_OFF + h * DH:KT_OFF + (h + 1) * DH],
                                regE[:, VT_OFF + h * DH:VT_OFF + (h + 1) * DH],
                                start=(h < 2), stop=(h >= NH - 2), tile_position=(0, pb_))
            last.then_inc(pe.h, 1)
        prog.emit("tensor", f)
        st["sd_pe"] = pe.inc()

    def s_update():
        eng_waits("vector", pe=st["sd_pe"])
        vop(lambda v: v.tensor_tensor(S[:], S[:], gCt[:], op=ALU.mult))
        vop(lambda v: v.tensor_tensor(S[0:DH, 0:512], S[0:DH, 0:512], pB[0:DH, 0:512], op=ALU.add))
        SUP = vop(lambda v: v.tensor_tensor(S[DH:P, 0:512], S[DH:P, 0:512], pB[DH:P, 512:1024], op=ALU.add))
        st["pB_free"] = {"dve": SUP}
        ew("scalar", {"dve": SUP})
        st["sbf_ready"] = {"act": aop(lambda s: s.copy(Sbf[:], S[:]))}

    def sdelta_supdate():
        sdelta_mm()
        s_update()

    def prefix_chunk(pidx):
        if pidx == 0:
            x_prep(xpbf_scr[0:P, :], CAST["xp0"], 0)
        if pidx + 1 < NPREF:
            q = pidx + 1
            x_prep(xpbf_scr[q * P:(q + 1) * P, :], CAST[f"xp{q // 4}"], q % 2)
        v_proj(pidx % 2)
        def f(t, wd=dmax(st["xfm_ready"], st["pe_pf_free"][0], st["pe_pf_free"][1]), wq=st["wa"]["wk"],
              sl=pidx % 2):
            twd(t, wd)
            t.wait_ge(sems["q_" + wq[0]].h, wq[1]) if isinstance(wq, tuple) else t.wait_ge(q_wa.h, wq)
            last = None
            for nh2, bank in ((0, pE), (1, pF)):
                for kc in range(KC):
                    last = t.matmul(bank,
                                    regE[:, XF_OFF + kc * 512 + sl * P:XF_OFF + kc * 512 + (sl + 1) * P],
                                    regC[:, WK_OFF + kc * D + nh2 * 512: WK_OFF + kc * D + (nh2 + 1) * 512],
                                    start=(kc == 0), stop=(kc == KC - 1))
            last.then_inc(pe.h, 1)
        prog.emit("tensor", f)
        K_PE = pe.inc()
        eng_waits("vector", pe=K_PE)
        st["pe_pf_free"][0] = {"dve": vop(lambda v: v.tensor_tensor(
            regE[:, KT_OFF:KT_OFF + 512], pE, decK[:, 0:512], op=ALU.mult))}
        kt = vop(lambda v: v.tensor_tensor(
            regE[:, KT_OFF + 512:KT_OFF + D], pF, decK[:, 512:D], op=ALU.mult))
        st["pe_pf_free"][1] = {"dve": kt}
        st["kt_ready"] = {"dve": kt}
        sdelta_supdate()

    def qk_proj():
        last_ev = 0
        for woff, foff in ((WQ_OFF, QF_OFF), (WK_OFF, KF_OFF)):
            wkey = "wq" if woff == WQ_OFF else "wk"
            for nt in range(KC):
                bidx = nt % 2
                bank = pE if bidx == 0 else pF
                def f(t, woff=woff, nt=nt, bank=bank,
                      wd=dmax(st["pe_pf_free"][bidx], st["xfm_ready"]), wq=st["wa"][wkey]):
                    twd(t, wd)
                    t.wait_ge(sems["q_" + wq[0]].h, wq[1]) if isinstance(wq, tuple) else t.wait_ge(q_wa.h, wq)
                    last = None
                    for kc in range(KC):
                        last = t.matmul(bank,
                                        regC[:, woff + kc * D + nt * P: woff + kc * D + (nt + 1) * P],
                                        regE[:, XF_OFF + kc * 512:XF_OFF + (kc + 1) * 512],
                                        start=(kc == 0), stop=(kc == KC - 1))
                    last.then_inc(pe.h, 1)
                prog.emit("tensor", f)
                QK_PE = pe.inc()
                ew("scalar", {"pe": QK_PE})
                last_ev = aop(lambda s, foff=foff, nt=nt, bank=bank: s.copy(
                    regD[:, foff + nt * 512:foff + (nt + 1) * 512], bank[:]))
                st["pe_pf_free"][bidx] = {"act": last_ev}
        st["qk_ready"] = {"act": last_ev}

    ebuf = stbf[0]
    ebuf_full = regG[:, D:2 * D]

    def front_early(g, tb, cloc):
        tc = tb * CPT + cloc
        xb = tc % 2
        ow = {("q_o0" if xb == 0 else "q_o1"): st["outsb_dma"][xb]}
        eng_waits("sync", dve=st["xtm_free_dve"][xb], **ow)
        st["xd"][xb] = sync_dma(lambda s, g=g, tc=tc, xb=xb: s.dma_start(
            out=xtm[xb][:], in_=x_d[(g * Tg + tc * P):(g * Tg + (tc + 1) * P), :]), q_x[xb])
        eng_waits("vector", **{("q_x0" if xb == 0 else "q_x1"): st["xd"][xb]})
        st["xbob"] = {"dve": vop(lambda v, xb=xb: v.tensor_tensor(
            xtm[xb][:], xtm[xb][:], bob[:], op=ALU.add))}

        v_proj(cloc)

        def f(t, wd=dmax(st["xfm_ready"], st["pB_free"]), cloc=cloc, wq=st["wa"]["wg"]):
            twd(t, wd)
            t.wait_ge(sems["q_" + wq[0]].h, wq[1]) if isinstance(wq, tuple) else t.wait_ge(q_wa.h, wq)
            last = None
            for nh2 in range(2):
                for kc in range(KC):
                    last = t.matmul(pB[:, nh2 * 512:(nh2 + 1) * 512],
                                    regE[:, XF_OFF + kc * 512 + cloc * P: XF_OFF + kc * 512 + (cloc + 1) * P],
                                    regC[:, WG_OFF + kc * D + nh2 * 512: WG_OFF + kc * D + (nh2 + 1) * 512],
                                    start=(kc == 0), stop=(kc == KC - 1))
            last.then_inc(pe.h, 1)
        prog.emit("tensor", f)
        G_PE = pe.inc()
        eng_waits("vector", pe=G_PE)
        vop(lambda v: v.tensor_tensor(pB[:, 0:512], pB[:, 0:512], bgb[:, 0:512], op=ALU.add))
        Z1 = vop(lambda v: v.tensor_tensor(pB[:, 512:1024], pB[:, 512:1024], bgb[:, 512:1024], op=ALU.add))
        eng_waits("scalar", dve=max(Z1, st["ea_ebuf_dve"]))
        SIL = aop(lambda s: s.activation(lnG[:], pB[:], AF.Silu))
        eng_waits("vector", act=SIL)
        GG = vop(lambda v: v.tensor_tensor(regE[:, GT_OFF:GT_OFF + D], lnG[:], gamb[:], op=ALU.mult))
        BG = vop(lambda v: v.tensor_tensor(lnG[:], lnG[:], betb[:], op=ALU.mult))
        st["pB_free"] = {"act": SIL}
        st["gG_ready"] = {"dve": GG}
        st["bG_ready"] = {"dve": BG}

        def f(t, wd=dmax(st["pc_free"], st["qk_ready"]), cloc=cloc):
            twd(t, wd)
            last = None
            for nt in range(KC):
                last = t.transpose(pC[:, nt * P:(nt + 1) * P],
                                   regD[:, KF_OFF + nt * 512 + cloc * P: KF_OFF + nt * 512 + (cloc + 1) * P],
                                   idn_b[:])
            last.then_inc(pe.h, 1)
        prog.emit("tensor", f)
        KT_PE = pe.inc()
        eng_waits("vector", pe=KT_PE)
        kt = vop(lambda v: v.tensor_tensor(
            regE[:, KT_OFF:KT_OFF + D], pC[:], decK[:], op=ALU.mult))
        st["pc_free"] = {"dve": kt}
        st["kt_ready"] = {"dve": kt}

        st["qd_ready"] = {"dve": vop(lambda v, cloc=cloc: v.tensor_tensor(
            regE[:, QD_OFF:QD_OFF + D],
            regD[:, QF_OFF:QF_OFF + KC * 512].rearrange("p (nt t) -> p nt t", nt=KC)[:, :, cloc * P:(cloc + 1) * P],
            decQ[:].rearrange("p (nt t) -> p nt t", nt=KC), op=ALU.mult))}

        sdelta_mm()

    def front_late(g, tb, cloc):
        ret_started = [False, False]
        last_ret_pe = 0
        for b4 in range(4):
            bidx = b4 % 2
            def f(t, b4=b4, wd=dmax(st["pe_pf_free"][0], st["pe_pf_free"][1], st["qk_ready"]), cloc=cloc):
                twd(t, wd)
                last = None
                for j in range(4):
                    h = b4 * 4 + j
                    nt = h // 2
                    pbase = (h % 2) * DH
                    bank = pE if j % 2 == 0 else pF
                    col = (j // 2) * P
                    lhsT = regD[pbase:pbase + DH, KF_OFF + nt * 512 + cloc * P: KF_OFF + nt * 512 + (cloc + 1) * P]
                    rhs = regD[pbase:pbase + DH, QF_OFF + nt * 512 + cloc * P: QF_OFF + nt * 512 + (cloc + 1) * P]
                    last = t.matmul(bank[:, col:col + P], lhsT, rhs, start=True, stop=True)
                last.then_inc(pe.h, 1)
            prog.emit("tensor", f)
            SC_PE = pe.inc()
            eng_waits("vector", pe=SC_PE)
            src = bass.AP(pEF, 0, [[pEF.shape[1], P], [512, 2], [P, 2], [1, P]])
            dst = bass.AP(regG, D + bidx * 512, [[3 * D, P], [P, 2], [2 * P, 2], [1, P]])
            msk = bass.AP(maskD, b4 * 4 * P, [[NH * P, P], [P, 2], [2 * P, 2], [1, P]])
            ST_DVE = vop(lambda v, src=src, dst=dst, msk=msk: v.tensor_tensor(dst, src, msk, op=ALU.mult))
            st["pe_pf_free"][0] = {"dve": ST_DVE}
            st["pe_pf_free"][1] = {"dve": ST_DVE}
            s0, s1 = ret_started
            def f(t, b4=b4, bidx=bidx,
                  wd=dmax({"dve": ST_DVE}, st["vt_ready"], st["qd_ready"], st["sbf_ready"], st["pA_free"]),
                  s0=s0, s1=s1):
                twd(t, wd)
                started = [s0, s1]
                last = None
                for j in range(4):
                    h = b4 * 4 + j
                    bk = 0 if h < 8 else 1
                    dst = pA[:, h * DH:(h + 1) * DH]
                    last = t.matmul(dst, stbf[bidx][:, j * P:(j + 1) * P],
                                    regE[:, VT_OFF + h * DH:VT_OFF + (h + 1) * DH],
                                    start=(not started[bk]), stop=False)
                    started[bk] = True
                    pbase = (h % 2) * DH
                    qd = regE[pbase:pbase + DH, QD_OFF + (h // 2) * P: QD_OFF + (h // 2 + 1) * P]
                    sbf_sl = Sbf[pbase:pbase + DH, (h // 2) * DH:(h // 2 + 1) * DH]
                    last = t.matmul(dst, qd, sbf_sl, start=False, stop=(h in (7, 15)))
                last.then_inc(pe.h, 1)
            prog.emit("tensor", f)
            last_ret_pe = pe.inc()
            if b4 < 2:
                ret_started[0] = True
            else:
                ret_started[1] = True
            if b4 == 1:
                eng_waits("vector", pe=last_ret_pe)
                vop(lambda v: v.bn_stats(bnbuf[:, 0:6], pA[:, 0:512]))
        st["st_free_pe_last"] = last_ret_pe
        eng_waits("vector", pe=last_ret_pe)
        vop(lambda v: v.bn_stats(bnbuf[:, 6:12], pA[:, 512:1024]))
        s_update()

    def epi_a(g, tb, cloc):
        BN = vop(lambda v: v.bn_aggr(mvar[:], bnbuf[:].rearrange("p (n t) -> p n t", t=3)))
        eng_waits("scalar", dve=BN)
        aop(lambda s: s.activation(lnv[:], mvar[:, 1:2], AF.Ln, bias=epsi[:]))
        RS = aop(lambda s: s.activation(rstd[:], lnv[:], AF.Exp, scale=-0.5))
        eng_waits("vector", act=RS, pe=st["prep_xt_pe"], q_xe=st["prep_q_xe"])
        TS1 = vop(lambda v: v.tensor_scalar(ebuf_full, pA[:], mvar[:, 0:1], rstd[:], op0=ALU.subtract, op1=ALU.mult))
        st["pA_free"] = {"dve": TS1}
        vop(lambda v: v.tensor_tensor(ebuf_full, ebuf_full, regE[:, GT_OFF:GT_OFF + D], op=ALU.mult))
        EAE = vop(lambda v: v.tensor_tensor(xy1bfc[:], ebuf_full, lnG[:], op=ALU.add))
        st["lnG_ready"] = {"dve": EAE}
        st["ea_ebuf_dve"] = EAE

    def epi_b(g, tb, cloc):
        tc = tb * CPT + cloc
        xb = tc % 2
        def f(t, wd=dmax(st["lnG_ready"], st["pc_free"], {"pe": st["xbfc_free_pe"]})):
            twd(t, wd)
            last = None
            for kc in range(KC):
                last = t.transpose(pC[:, kc * P:(kc + 1) * P], xy1bfc[:, kc * P:(kc + 1) * P], idn_b[:])
            last.then_inc(pe.h, 1)
        prog.emit("tensor", f)
        LT_PE = pe.inc()
        st["xbfc_free_pe"] = LT_PE
        ew("scalar", {"pe": LT_PE})
        LNGF = aop(lambda s, cloc=cloc: s.copy(
            regE[:, XF_OFF:XF_OFF + KC * 512].rearrange("p (kc t) -> p kc t", kc=KC)[:, :, cloc * P:(cloc + 1) * P],
            pC[:].rearrange("p (kc t) -> p kc t", kc=KC)))
        st["pc_free"] = {"act": LNGF}

        def f(t, wd=dmax({"act": LNGF}, st["pe_pf_free"][0], st["pe_pf_free"][1]), wq=st["wa"]["wo"], cloc=cloc):
            twd(t, wd)
            t.wait_ge(sems["q_" + wq[0]].h, wq[1]) if isinstance(wq, tuple) else t.wait_ge(q_wa.h, wq)
            last = None
            for nh2, bank in ((0, pE), (1, pF)):
                for kc in range(KC):
                    last = t.matmul(bank,
                                    regE[:, XF_OFF + kc * 512 + cloc * P: XF_OFF + kc * 512 + (cloc + 1) * P],
                                    regD[:, WO_OFF + kc * D + nh2 * 512: WO_OFF + kc * D + (nh2 + 1) * 512],
                                    start=(kc == 0), stop=(kc == KC - 1))
            last.then_inc(pe.h, 1)
        prog.emit("tensor", f)
        AO_PE = pe.inc()
        st["ao_pe"] = AO_PE

        eng_waits("vector", pe=AO_PE)
        st["pe_pf_free"][0] = {"dve": vop(lambda v, xb=xb: v.tensor_tensor(
            t1r1[:, 0:512], pE, xtm[xb][:, 0:512], op=ALU.add))}
        R1X = vop(lambda v, xb=xb: v.tensor_tensor(
            t1r1[:, 512:1024], pF, xtm[xb][:, 512:1024], op=ALU.add))
        st["pe_pf_free"][1] = {"dve": R1X}
        st["xtm_free_dve"][xb] = R1X
        vop(lambda v: v.bn_stats(bnbuf[:, 0:6], t1r1[:, 0:512]))
        vop(lambda v: v.bn_stats(bnbuf[:, 6:12], t1r1[:, 512:1024]))
        BN2 = vop(lambda v: v.bn_aggr(mvar[:], bnbuf[:].rearrange("p (n t) -> p n t", t=3)))
        eng_waits("scalar", dve=BN2)
        aop(lambda s: s.activation(lnv[:], mvar[:, 1:2], AF.Ln, bias=eps5[:]))
        RS2 = aop(lambda s: s.activation(rstd[:], lnv[:], AF.Exp, scale=-0.5))
        eng_waits("vector", act=RS2)
        NB2 = vop(lambda v: v.tensor_scalar(nbias[:], mvar[:, 0:1], rstd[:], -1.0, op0=ALU.mult, op1=ALU.mult))
        eng_waits("scalar", dve=NB2)
        ACC_W = aop(lambda s, tc=tc: s.activation(acc[:, tc * D:(tc + 1) * D], t1r1[:],
                                                  AF.Identity, bias=nbias[:], scale=rstd[:]))

        ew("scalar", {"pe": st["xbfc_free_pe"]})
        Y1C = aop(lambda s, tc=tc: s.copy(xy1bfc[:], acc[:, tc * D:(tc + 1) * D]))
        def f(t, wd=dmax({"act": Y1C}, st["pc_free"])):
            twd(t, wd)
            last = None
            for kc in range(KC):
                last = t.transpose(pC[:, kc * P:(kc + 1) * P], xy1bfc[:, kc * P:(kc + 1) * P], idn_b[:])
            last.then_inc(pe.h, 1)
        prog.emit("tensor", f)
        YT_PE = pe.inc()
        st["xbfc_free_pe"] = YT_PE
        ew("scalar", {"pe": YT_PE})
        Y1F = aop(lambda s, tc=tc: s.copy(
            y1fm[:].rearrange("p (kc t) -> p kc t", kc=KC)[:, :, tc * P:(tc + 1) * P],
            pC[:].rearrange("p (kc t) -> p kc t", kc=KC)))
        st["pc_free"] = {"act": Y1F}
        st["y1fm_ready"] = {"act": Y1F}

        def f(t, wd=dmax({"act": Y1F}, st["pW_free"]), wq=WR_READY, tc=tc, cloc=cloc):
            twd(t, wd)
            t.wait_ge(sems["q_" + wq[0]].h, wq[1]) if isinstance(wq, tuple) else t.wait_ge(q_wa.h, wq)
            last = None
            for kc in range(KC):
                last = t.matmul(pW[:, cloc * E:(cloc + 1) * E],
                                y1fm[:].rearrange("p (kc t) -> p kc t", kc=KC)[:, kc, tc * P:(tc + 1) * P],
                                wrsb[:, kc * E:(kc + 1) * E], start=(kc == 0), stop=(kc == KC - 1))
            last.then_inc(pe.h, 1)
        prog.emit("tensor", f)
        st["rl_pe"] = pe.inc()

    def router_tblock(g, tb):
        v3 = lambda ap: ap.rearrange("p (c e) -> p c e", c=CPT)
        bc = lambda t4: bass.AP(t4, 0, [[CPT, P], [1, CPT], [0, E]])
        eng_waits("vector", pe=st["rl_pe"])
        vop(lambda v: v.tensor_tensor(
            rl[:].rearrange("p (c e) -> p c e", c=CPT),
            pW[:, 0:CPT * E].rearrange("p (c e) -> p c e", c=CPT),
            bass.AP(brb, 0, [[E, P], [0, CPT], [1, E]]), op=ALU.add))
        RL_D = vop(lambda v: v.tensor_reduce(rm1[:], v3(rl[:]), op=ALU.max, axis=AX.X))
        st["pW_free"] = {"dve": RL_D}
        vop(lambda v: v.tensor_tensor(req[:].rearrange("p (c e) -> p c e", c=CPT),
                                      v3(rl[:]), bc(rm1), op=ALU.is_equal))
        vop(lambda v: v.scalar_tensor_tensor(rl2[:], req[:], -1e30, rl[:], op0=ALU.mult, op1=ALU.add))
        vop(lambda v: v.tensor_reduce(rm2[:], v3(rl2[:]), op=ALU.max, axis=AX.X))
        vop(lambda v: v.tensor_tensor(rge[:].rearrange("p (c e) -> p c e", c=CPT),
                                      v3(rl[:]), bc(rm2), op=ALU.is_ge))
        SUB = vop(lambda v: v.tensor_tensor(rl2[:].rearrange("p (c e) -> p c e", c=CPT),
                                            v3(rl[:]), bc(rm1), op=ALU.subtract))
        eng_waits("scalar", dve=SUB)
        REXP = aop(lambda s: s.activation(rexp[:], rl2[:], AF.Exp))
        eng_waits("vector", act=REXP)
        vop(lambda v: v.tensor_tensor(rexp[:], rexp[:], rge[:], op=ALU.mult))
        vop(lambda v: v.tensor_reduce(rsum[:], v3(rexp[:]), op=ALU.add, axis=AX.X))
        vop(lambda v: v.reciprocal(rrs[:], rsum[:]))
        WSB = vop(lambda v, tb=tb: v.tensor_tensor(
            wsb[:, tb * CPT * E:(tb + 1) * CPT * E].rearrange("p (c e) -> p c e", c=CPT),
            v3(rexp[:]), bc(rrs), op=ALU.mult))
        return WSB


    w1_ready = {}
    def w1_load(key, e, half, slab, extra_pe=0):
        buf = slab % 2
        wait_pe = max(st["w1_free_pe"][buf], extra_pe)
        sem = q_w1[buf]
        castc = CAST[f"w1_{e}"]
        sg = half * N_SLAB + slab
        def f(s_, e=e, sg=sg, buf=buf, wait_pe=wait_pe, sem=sem, castc=castc):
            s_.wait_ge(q_cast.h, castc)
            if wait_pe:
                s_.wait_ge(pe.h, wait_pe)
            s_.dma_start(out=w1buf[buf].rearrange("p (kc h) -> p kc h", kc=KC),
                         in_=w1_scr[e * P:(e + 1) * P, :].rearrange(
                             "p (kc h) -> p kc h", kc=KC)[:, :, sg * SLAB_H:(sg + 1) * SLAB_H]
                         ).then_inc(sem.h, 16)
        prog.emit("sync", f)
        w1_ready[key] = (buf, sem.inc(16))

    w2_ready = {}
    def w2_load(key, e, wait_pe):
        castc = CAST[f"w2_{e}"]
        def f(g_, e=e, wait_pe=wait_pe, castc=castc):
            g_.wait_ge(q_cast.h, castc)
            if wait_pe:
                g_.wait_ge(pe.h, wait_pe)
            g_.dma_start(out=w2buf[:], in_=w2_scr[e * P:(e + 1) * P, :]).then_inc(q_w2.h, 16)
        prog.emit("gpsimd", f)
        w2_ready[key] = q_w2.inc(16)

    def ffn_group(g, wsb_ready, acc_ready_act, y1fm_ready):
        WF_EV = 0
        for rnd in range(2):
            def f(t, wd=dmax({"dve": max(wsb_ready, WF_EV)}, st["pW_free"], {"pe": st["wfm_reader_pe"]}), rnd=rnd):
                twd(t, wd)
                last = None
                for i in range(4):
                    tc = rnd * 4 + i
                    last = t.transpose(pW[0:E, i * P:(i + 1) * P], wsb[:, tc * E:(tc + 1) * E], idn_f[:])
                last.then_inc(pe.h, 1)
            prog.emit("tensor", f)
            WT_PE = pe.inc()
            eng_waits("vector", pe=WT_PE)
            vop(lambda v, rnd=rnd: v.tensor_copy(wfmhi[0:E, rnd * 512:(rnd + 1) * 512], pW[0:E, 0:512]))
            WF_EV = vop(lambda v, rnd=rnd: v.tensor_tensor(
                wfmlo[0:E, rnd * 512:(rnd + 1) * 512], pW[0:E, 0:512],
                wfmhi[0:E, rnd * 512:(rnd + 1) * 512], op=ALU.subtract))
        st["pW_free"] = {"dve": WF_EV}
        st["ph_free"][2] = dmax(st["ph_free"][2], {"dve": WF_EV})
        WFM_READY = WF_EV

        po_free = [dmax({"dve": po_free_dve[0]}, st["pA_free"]),
                   dmax({"dve": po_free_dve[1]}, st["pB_free"])]
        last_b2_pe = 0
        for tc in range(TC):
            pb = tc % 2
            bank = pA if pb == 0 else pB
            def f(t, tc=tc, bank=bank, wd=dmax({"dve": max(WFM_READY, PREP)}, po_free[pb], st["pA_free"])):
                twd(t, wd)
                last = None
                for nh in range(2):
                    dst = bank[:, nh * 512:(nh + 1) * 512]
                    for i, (lhs, rhs) in enumerate(((wfmhi, b2hi), (wfmhi, b2lo), (wfmlo, b2hi))):
                        last = t.matmul(dst, lhs[0:E, tc * P:(tc + 1) * P],
                                        rhs[0:E, nh * 512:(nh + 1) * 512],
                                        start=(i == 0), stop=(i == 2))
                last.then_inc(pe.h, 1)
            prog.emit("tensor", f)
            last_b2_pe = pe.inc()
            eng_waits("vector", pe=last_b2_pe, act=acc_ready_act)
            po_free_dve[pb] = vop(lambda v, tc=tc, bank=bank: v.tensor_tensor(
                acc[:, tc * D:(tc + 1) * D], acc[:, tc * D:(tc + 1) * D], bank[:], op=ALU.add))
            po_free[pb] = {"dve": po_free_dve[pb]}
        st["wfm_reader_pe"] = last_b2_pe

        pepf0 = [dict(st["pe_pf_free"][0]), dict(st["pe_pf_free"][1])]
        for e in range(E):
            if g == 0 and e + 1 < E:
                cast_w1(e + 1)
                cast_w2(e + 1)
            for half in range(NHALF):
                evac_start = act.n
                gelu_cnt = [0] * HCH
                for hcl in range(HCH):
                    slab = hcl // HC_PER_SLAB
                    key = (g, e, half, slab)
                    if key not in w1_ready:
                        w1_load(key, e, half, slab)
                    for th in range(2):
                        it = hcl * 2 + th
                        b = (half * HCH * 2 + it) % 3
                        bank = (pE, pF, pW[:])[b]
                        wqbuf, wqcnt = w1_ready[key]
                        def f(t, e=e, half=half, hcl=hcl, th=th, b=b, bank=bank,
                              wqbuf=wqbuf, wqcnt=wqcnt,
                              wd=dmax(st["ph_free"][b], y1fm_ready, pepf0[b % 2])):
                            t.wait_ge(q_w1[wqbuf].h, wqcnt)
                            twd(t, wd)
                            last = None
                            hloc = hcl % HC_PER_SLAB
                            wb_ = w1buf[(hcl // HC_PER_SLAB) % 2]
                            for kc in range(KC):
                                lhsT = wb_[:, kc * SLAB_H + hloc * P: kc * SLAB_H + (hloc + 1) * P]
                                rhs = y1fm[:, kc * Tg + th * 512: kc * Tg + (th + 1) * 512]
                                last = t.matmul(bank, lhsT, rhs, start=(kc == 0), stop=(kc == KC - 1))
                            last.then_inc(pe.h, 1)
                        prog.emit("tensor", f)
                        mm_pe = pe.inc()
                        if hcl % HC_PER_SLAB == HC_PER_SLAB - 1 and th == 1:
                            st["w1_free_pe"][(hcl // HC_PER_SLAB) % 2] = mm_pe
                            nslab = slab + 2
                            if nslab < N_SLAB:
                                w1_load((g, e, half, nslab), e, half, nslab)
                            elif half + 1 < NHALF:
                                w1_load((g, e, half + 1, nslab - N_SLAB), e, half + 1, nslab - N_SLAB)
                            elif e + 1 < E:
                                w1_load((g, e + 1, 0, nslab - N_SLAB), e + 1, 0, nslab - N_SLAB)
                        eng_waits("scalar", pe=mm_pe)
                        gc = aop(lambda s, e=e, half=half, hcl=hcl, th=th, bank=bank: s.activation(
                            hbuf[:, hcl * Tg + th * 512: hcl * Tg + (th + 1) * 512],
                            bank, AF.Gelu,
                            bias=b1fm[:, e * HC + half * HCH + hcl: e * HC + half * HCH + hcl + 1]))
                        st["ph_free"][b] = {"act": gc}
                        gelu_cnt[hcl] = gc
                evac_end = act.n
                for tc in range(TC):
                    pb = tc % 2
                    bank = pA if pb == 0 else pB
                    def f(t, e=e, half=half, tc=tc, bank=bank,
                          wact=evac_end, ww2=w2_ready[(g, e)], wd=dict(po_free[pb])):
                        t.wait_ge(act.h, wact)
                        t.wait_ge(q_w2.h, ww2)
                        twd(t, wd)
                        last = None
                        for nh in range(2):
                            dst = bank[:, nh * 512:(nh + 1) * 512]
                            for hcl in range(HCH):
                                hc = half * HCH + hcl
                                lhsT = hbuf[:, hcl * Tg + tc * P: hcl * Tg + (tc + 1) * P]
                                rhs = w2buf[:, hc * D + nh * 512: hc * D + (nh + 1) * 512]
                                last = t.matmul(dst, lhsT, rhs, start=(hcl == 0), stop=(hcl == HCH - 1))
                        last.then_inc(pe.h, 1)
                    prog.emit("tensor", f)
                    mm_pe = pe.inc()
                    if tc == TC - 1 and half == NHALF - 1:
                        st["w2_free_pe"] = mm_pe
                    eng_waits("vector", pe=mm_pe)
                    po_free_dve[pb] = vop(lambda v, e=e, tc=tc, bank=bank: v.scalar_tensor_tensor(
                        acc[:, tc * D:(tc + 1) * D], bank[:], wsb[:, tc * E + e: tc * E + e + 1],
                        acc[:, tc * D:(tc + 1) * D], op0=ALU.mult, op1=ALU.add))
                    po_free[pb] = {"dve": po_free_dve[pb]}
            if e + 1 < E:
                w2_load((g, e + 1), e + 1, st["w2_free_pe"])
        st["ffn_last_pe"] = pe.n
        st["pA_free"] = {"dve": po_free_dve[0]}
        st["pB_free"] = {"dve": po_free_dve[1]}
        st["pW_free"] = dmax(st["pW_free"], st["ph_free"][2])
        st["xbfc_free_pe"] = max(st["xbfc_free_pe"], st["wfm_reader_pe"])

        for tc in range(TC):
            vop(lambda v, tc=tc: v.bn_stats(bnbuf[:, 0:6], acc[:, tc * D: tc * D + 512]))
            vop(lambda v, tc=tc: v.bn_stats(bnbuf[:, 6:12], acc[:, tc * D + 512: tc * D + 1024]))
            BN3 = vop(lambda v: v.bn_aggr(mvar[:], bnbuf[:].rearrange("p (n t) -> p n t", t=3)))
            eng_waits("scalar", dve=BN3)
            aop(lambda s: s.activation(lnv[:], mvar[:, 1:2], AF.Ln, bias=eps5[:]))
            RS3 = aop(lambda s: s.activation(rstd[:], lnv[:], AF.Exp, scale=-0.5))
            eng_waits("vector", act=RS3)
            NB3 = vop(lambda v: v.tensor_scalar(nbias[:], mvar[:, 0:1], rstd[:], -1.0, op0=ALU.mult, op1=ALU.mult))
            ob = tc % 2
            prev = st["outsb_dma"][ob]
            eng_waits("scalar", dve=NB3, **{("q_o0" if ob == 0 else "q_o1"): prev})
            OA = aop(lambda s, tc=tc, ob=ob: s.activation(outsb[ob][:], acc[:, tc * D:(tc + 1) * D],
                                                          AF.Identity, bias=nbias[:], scale=rstd[:]))
            def f(s_, g=g, tc=tc, ob=ob, OA=OA):
                s_.wait_ge(act.h, OA)
                s_.dma_start(out=y_d[g * Tg + tc * P: g * Tg + (tc + 1) * P, :],
                             in_=outsb[ob][:]).then_inc(q_o[ob].h, 16)
            prog.emit("sync", f)
            st["outsb_dma"][ob] = q_o[ob].inc(16)

    for pidx in range(NPREF):
        prefix_chunk(pidx)

    if DEBUG_STAGE == "prefix":
        SS = st["sbf_ready"].get("dve", 0)
        SSA = st["sbf_ready"].get("act", 0)
        def fdbg(s_, SS=SS, SSA=SSA):
            if SS:
                s_.wait_ge(dve.h, SS)
            if SSA:
                s_.wait_ge(act.h, SSA)
            s_.dma_start(out=y_d[0:P, 0:512], in_=S[:]).then_inc(q_o0.h, 16)
            s_.wait_ge(q_o0.h, 16)
        prog.emit("sync", fdbg)
        _assemble(nc, prog, ctx)
        return nc

    for g in range(NG):
        wsb_last = 0
        for tb in range(NTB):
            if tb == 0:
                for cloc in range(CPT):
                    tc = cloc
                    gtc = g * TC + tc
                    x_prep(xbf_scr[(g * Tg + tc * P):(g * Tg + (tc + 1) * P), :], CAST[f"x{gtc // 4}"], cloc)
            qk_proj()
            front_early(g, tb, 0)
            for cloc in range(CPT):
                front_late(g, tb, cloc)
                epi_a(g, tb, cloc)
                if cloc + 1 < CPT:
                    front_early(g, tb, cloc + 1)
                epi_b(g, tb, cloc)
                if tb == 0:
                    tc1 = CPT + cloc
                    gtc1 = g * TC + tc1
                    x_prep(xbf_scr[(g * Tg + tc1 * P):(g * Tg + (tc1 + 1) * P), :], CAST[f"x{gtc1 // 4}"],
                           cloc, stage=ebuf_full,
                           extra_ld={"dve": st["ea_ebuf_dve"], "pe": st["st_free_pe_last"]},
                           slot_pe=st["ao_pe"])
            wsb_last = router_tblock(g, tb)
        attn_end_pe = pe.n
        if DEBUG_STAGE == "attn":
            AC = act.n
            WS = dve.n
            for tc in range(TC):
                def fdbg(s_, g=g, tc=tc, AC=AC, WS=WS):
                    s_.wait_ge(act.h, AC)
                    s_.wait_ge(dve.h, WS)
                    s_.dma_start(out=y_d[g * Tg + tc * P: g * Tg + (tc + 1) * P, :],
                                 in_=acc[:, tc * D:(tc + 1) * D]).then_inc(q_o0.h, 16)
                prog.emit("sync", fdbg)
                q_o0.inc(16)
            st["pA_free"] = dmax(st["pA_free"], {"dve": dve.n})
            st["pB_free"] = dmax(st["pB_free"], {"dve": dve.n})
            continue
        w2_load((g, 0), 0, attn_end_pe)
        w1_load((g, 0, 0, 0), 0, 0, 0, extra_pe=attn_end_pe)
        w1_load((g, 0, 0, 1), 0, 0, 1, extra_pe=attn_end_pe)
        ffn_group(g, wsb_last, act.n, dict(st["y1fm_ready"]))
        if g + 1 < NG:
            w_attn_loads(st["ffn_last_pe"])

    def fin(s_):
        s_.wait_ge(q_o0.h, q_o0.n)
        s_.wait_ge(q_o1.h, q_o1.n)
    prog.emit("sync", fin)

    _assemble(nc, prog, ctx)
    return nc


def _assemble(nc, prog, ctx):
    with nc.Block() as block:
        @block.sync
        def _(s_):
            for fn in prog.ops["sync"]:
                fn(s_)
        @block.gpsimd
        def _(g_):
            for fn in prog.ops["gpsimd"]:
                fn(g_)
        @block.tensor
        def _(t_):
            for fn in prog.ops["tensor"]:
                fn(t_)
        @block.vector
        def _(v_):
            for fn in prog.ops["vector"]:
                fn(v_)
        @block.scalar
        def _(s_):
            for fn in prog.ops["scalar"]:
                fn(s_)
    ctx.close()


def kernel(**inputs):
    x = np.ascontiguousarray(np.asarray(inputs["L0_attn_x"], dtype=np.float32))
    consts = host_consts()
    zeros_pref = np.zeros((NPREF * P, D), np.float32)
    base = {
        "Wq": inputs["L0_attn_Wq"], "Wk": inputs["L0_attn_Wk"], "Wv": inputs["L0_attn_Wv"],
        "Wg": inputs["L0_attn_Wg"], "Wo": inputs["L0_attn_Wo"],
        "W1": inputs["L0_ffn_W1"], "b1": inputs["L0_ffn_b1"],
        "W2": inputs["L0_ffn_W2"], "b2": inputs["L0_ffn_b2"],
        "Wr": inputs["L0_ffn_Wr"],
    }
    base = {k: np.ascontiguousarray(np.asarray(v, dtype=np.float32)) for k, v in base.items()}
    bcast = lambda v, n: np.ascontiguousarray(np.broadcast_to(np.asarray(v, np.float32), (P, n)))
    base["bg_b"] = bcast(inputs["L0_attn_bg"], D).astype(ml_dtypes.bfloat16)
    base["bo_b"] = bcast(inputs["L0_attn_bo"], D).astype(ml_dtypes.bfloat16)
    base["gam_b"] = bcast(inputs["L0_attn_ln_gamma"], D).astype(ml_dtypes.bfloat16)
    base["bet_b"] = bcast(inputs["L0_attn_ln_beta"], D).astype(ml_dtypes.bfloat16)
    base["br_b"] = bcast(inputs["L0_ffn_br"], E)
    base.update(consts)

    in_maps = []
    for c in range(8):
        b, half = c // 2, c % 2
        m = dict(base)
        m["x"] = np.ascontiguousarray(x[b, half * 4096:(half + 1) * 4096])
        m["xp"] = np.ascontiguousarray(x[b, 0:4096]) if half == 1 else zeros_pref
        in_maps.append(m)

    nc = build_program()
    res = run_bass_kernel_spmd(nc, in_maps, core_ids=list(range(8)))
    out = np.empty((4, 8192, 1024), np.float32)
    for c in range(8):
        b, half = c // 2, c % 2
        out[b, half * 4096:(half + 1) * 4096] = res.results[c]["y"]
    return out


# version 18 (final)
# speedup vs baseline: 1.1028x; 1.1028x over previous
import numpy as np
import ml_dtypes
import concourse.bass as bass
import concourse.mybir as mybir
from concourse.bass_utils import run_bass_kernel_spmd
from contextlib import ExitStack

F32 = mybir.dt.float32
BF16 = mybir.dt.bfloat16
AF = mybir.ActivationFunctionType
ALU = mybir.AluOpType
AX = mybir.AxisListType

P = 128
D = 1024
H = 4096
E = 8
NH = 16
DH = 64
KC = D // P
HC = H // P
Tg = 1024
TC = Tg // P
NTB = 2
CPT = 4
NG = 4
NHALF = 2
HCH = HC // NHALF
SLAB_H = 512
N_SLAB = (HCH * P) // SLAB_H
HC_PER_SLAB = SLAB_H // P
NPREF = 32
LN_EPS_IN = 1e-3
LN_EPS = 1e-5
DEBUG_STAGE = None


class Ctr:
    def __init__(self, handle):
        self.h = handle
        self.n = 0
    def inc(self, n=1):
        self.n += n
        return self.n


class Prog:
    ENGS = ("sync", "gpsimd", "tensor", "vector", "scalar")
    def __init__(self):
        self.ops = {k: [] for k in self.ENGS}
    def emit(self, eng, fn):
        self.ops[eng].append(fn)


def host_consts():
    g = 1.0 - 2.0 ** (-5.0 - np.arange(NH, dtype=np.float64))
    t = np.arange(P, dtype=np.float64)
    diff = t[None, :] - t[:, None]
    maskD = np.zeros((P, NH * P), np.float64)
    for h in range(NH):
        maskD[:, h * P:(h + 1) * P] = np.where(diff >= 0, g[h] ** np.maximum(diff, 0) / 8.0, 0.0)
    decQ = np.zeros((P, KC * P), np.float64)
    for i in range(KC):
        for hf in range(2):
            hh = 2 * i + hf
            decQ[hf * DH:(hf + 1) * DH, i * P:(i + 1) * P] = (g[hh] ** (t + 1.0) / 8.0)[None, :]
    decK = np.zeros((P, KC * P), np.float64)
    for i in range(KC):
        for hf in range(2):
            hh = 2 * i + hf
            decK[:, i * P + hf * DH: i * P + (hf + 1) * DH] = (g[hh] ** (127.0 - t))[:, None]
    gC = np.zeros((P, 512), np.float64)
    for h in range(NH):
        gC[DH * (h % 2):DH * (h % 2 + 1), (h // 2) * DH:(h // 2 + 1) * DH] = g[h] ** 128.0
    return {
        "maskD": np.ascontiguousarray(maskD.astype(ml_dtypes.bfloat16)),
        "decQ": np.ascontiguousarray(decQ.astype(ml_dtypes.bfloat16)),
        "decK": np.ascontiguousarray(decK.astype(ml_dtypes.bfloat16)),
        "gC": np.ascontiguousarray(gC.astype(np.float32)),
        "idn": np.eye(P, dtype=np.float32),
    }


def build_program():
    nc = bass.Bass("TRN2", target_bir_lowering=False)
    dt = lambda name, shape, dtype=F32, kind="ExternalInput": nc.dram_tensor(name, shape, dtype, kind=kind).ap()

    x_d = dt("x", [NG * Tg, D])
    xp_d = dt("xp", [NPREF * P, D])
    wq_d = dt("Wq", [D, D]); wk_d = dt("Wk", [D, D]); wv_d = dt("Wv", [D, D])
    wg_d = dt("Wg", [D, D]); wo_d = dt("Wo", [D, D])
    bgb_d = dt("bg_b", [P, D], BF16)
    bob_d = dt("bo_b", [P, D], BF16)
    gam_d = dt("gam_b", [P, D], BF16)
    bet_d = dt("bet_b", [P, D], BF16)
    w1_d = dt("W1", [E, D, H]); b1_d = dt("b1", [E, H])
    w2_d = dt("W2", [E, H, D]); b2_d = dt("b2", [E, D])
    wr_d = dt("Wr", [D, E]); brb_d = dt("br_b", [P, E])
    maskD_d = dt("maskD", [P, NH * P], BF16)
    decQ_d = dt("decQ", [P, KC * P], BF16)
    decK_d = dt("decK", [P, KC * P], BF16)
    gC_d = dt("gC", [P, 512])
    idn_d = dt("idn", [P, P])
    y_d = dt("y", [NG * Tg, D], F32, kind="ExternalOutput")
    xbf_scr = dt("xbf_scr", [NG * Tg, D], BF16, kind="Internal")
    xpbf_scr = dt("xpbf_scr", [NPREF * P, D], BF16, kind="Internal")
    wA_scr = dt("wA_scr", [5 * P, KC * D], BF16, kind="Internal")
    w1_scr = dt("w1_scr", [E * P, KC * H], BF16, kind="Internal")
    w2_scr = dt("w2_scr", [E * P, HC * D], BF16, kind="Internal")

    ctx = ExitStack()
    sb = lambda n, s, d=F32: ctx.enter_context(nc.sbuf_tensor(n, s, d))
    ps = lambda n, s, d=F32: ctx.enter_context(nc.psum_tensor(n, s, d))

    acc = sb("acc", [P, TC * D])
    y1fm = sb("y1fm", [P, KC * Tg], BF16)
    regC = sb("regC", [P, HC * D], BF16)
    w2buf = regC
    WQ_OFF, WK_OFF, WV_OFF, WG_OFF = 0, KC * D, 2 * KC * D, 3 * KC * D
    regD = sb("regD", [P, HCH * Tg], BF16)
    hbuf = regD
    WO_OFF = 0
    QF_OFF = KC * D
    KF_OFF = KC * D + KC * 512
    regE = sb("regE", [P, 2 * KC * SLAB_H], BF16)
    w1buf = [regE[:, i * KC * SLAB_H:(i + 1) * KC * SLAB_H] for i in range(2)]
    XF_OFF = 0
    VT_OFF = KC * 512
    GT_OFF = KC * 512 + D
    KT_OFF = KC * 512 + 2 * D
    QD_OFF = KC * 512 + 3 * D
    regF = sb("regF", [P, 3 * D])
    xtm = [regF[:, 0:D], regF[:, D:2 * D]]
    t1r1 = regF[:, 2 * D:3 * D]
    outsb = [regF[:, 0:D], regF[:, D:2 * D]]
    b1stg = regF[0:HC, 0:E * P]
    b2stg = regF[0:E, D:2 * D]
    regG = sb("regG", [P, 3 * D], BF16)
    lnG = regG[:, 0:D]
    stbf = [regG[:, D:D + 512], regG[:, D + 512:2 * D]]
    xy1bfc = regG[:, 2 * D:3 * D]
    wfmhi = regG[0:E, D:2 * D]
    wfmlo = regG[0:E, 2 * D:3 * D]
    maskD = sb("maskDs", [P, NH * P], BF16)
    decQ = sb("decQs", [P, KC * P], BF16)
    decK = sb("decKs", [P, KC * P], BF16)
    gCt = sb("gCt", [P, 512])
    gamb = sb("gamb", [P, D], BF16)
    betb = sb("betb", [P, D], BF16)
    bgb = sb("bgb", [P, D], BF16)
    bob = sb("bob", [P, D], BF16)
    idn_f = sb("idn_f", [P, P])
    idn_b = sb("idn_b", [P, P], BF16)
    b1fm = sb("b1fm", [P, E * HC])
    b2hi = sb("b2hi", [E, D], BF16)
    b2lo = sb("b2lo", [E, D], BF16)
    wrsb = sb("wrsb", [P, KC * E], BF16)
    brb = sb("brb", [P, E])
    S = sb("S", [P, 512])
    Sbf = sb("Sbf", [P, 512], BF16)
    wsb = sb("wsb", [P, TC * E])
    mvar = sb("mvar", [P, 2])
    bnbuf = sb("bnbuf", [P, 12])
    lnv = sb("lnv", [P, 1])
    rstd = sb("rstd", [P, 1])
    nbias = sb("nbias", [P, 1])
    epsi = sb("epsi", [P, 1])
    eps5 = sb("eps5", [P, 1])
    rl = sb("rl", [P, CPT * E])
    rl2 = sb("rl2", [P, CPT * E])
    rm1 = sb("rm1", [P, CPT])
    rm2 = sb("rm2", [P, CPT])
    req = sb("req", [P, CPT * E])
    rge = sb("rge", [P, CPT * E])
    rexp = sb("rexp", [P, CPT * E])
    rsum = sb("rsum", [P, CPT])
    rrs = sb("rrs", [P, CPT])

    pA = ps("pA", [P, D])
    pB = ps("pB", [P, D])
    pC = ps("pC", [P, D], BF16)
    pEF = ps("pEF", [P, 1024])
    pE = pEF[:, 0:512]
    pF = pEF[:, 512:1024]
    PE_QUARTER = pEF[:, 0:256]
    PF_QUARTER = pEF[:, 512:768]
    pW = ps("pW", [P, 512])

    sem_names = ("q_c", "q_x0", "q_x1", "q_xc", "q_xe", "q_wa", "q_w2", "q_w1a", "q_w1b", "q_o0", "q_o1", "q_cast", "pe", "act", "dve")
    sems = {k: Ctr(ctx.enter_context(nc.semaphore(k))) for k in sem_names}
    _WA_NAMES = ("wq", "wk", "wv", "wg", "wo")
    (q_c, q_x0, q_x1, q_xc, q_xe, q_wa, q_w2, q_w1a, q_w1b, q_o0, q_o1, q_cast, pe, act, dve) = (sems[k] for k in sem_names)
    q_x = [q_x0, q_x1]
    q_w1 = [q_w1a, q_w1b]
    q_o = [q_o0, q_o1]
    prog = Prog()

    def vop(fn, n_inc=1):
        wait_n = dve.n
        def f(v):
            v.wait_ge(dve.h, wait_n)
            fn(v).then_inc(dve.h, n_inc)
        prog.emit("vector", f)
        return dve.inc(n_inc)

    def aop(fn, n_inc=1):
        wait_n = act.n
        def f(s):
            s.wait_ge(act.h, wait_n)
            fn(s).then_inc(act.h, n_inc)
        prog.emit("scalar", f)
        return act.inc(n_inc)

    def eng_waits(eng, **kw):
        items = [(sems[k].h, v) for k, v in kw.items() if v > 0]
        if not items:
            return
        def f(e):
            for h, v in items:
                e.wait_ge(h, v)
        prog.emit(eng, f)

    def sync_dma(fn, sem):
        def f(s):
            fn(s).then_inc(sem.h, 16)
        prog.emit("sync", f)
        return sem.inc(16)

    def gp_dma(fn, sem, wait_pe=0):
        def f(g):
            if wait_pe:
                g.wait_ge(pe.h, wait_pe)
            fn(g).then_inc(sem.h, 16)
        prog.emit("gpsimd", f)
        return sem.inc(16)

    st = {
        "xtm_free_dve": [0, 0],
        "xbfc_free_pe": 0,
        "pc_free": {},
        "xfm_ready": {},
        "qk_ready": {},
        "vt_ready": {},
        "gt_ready": {},
        "kt_ready": {},
        "qd_ready": {},
        "sbf_ready": {},
        "pe_pf_free": [{}, {}],
        "pA_free": {},
        "pB_free": {},
        "pW_free": {},
        "y1fm_ready": {},
        "st_free_pe_last": 0,
        "ea_ebuf_dve": 0,
        "prep_xt_pe": 0,
        "prep_q_xe": 0,
        "rl_pe": 0,
        "xbob": {},
        "xd": [0, 0],
        "gG_ready": {},
        "bG_ready": {},
        "lnG_ready": {},
        "w1_free_pe": [0, 0],
        "w2_free_pe": 0,
        "wa": {},
        "wfm_reader_pe": 0,
        "ph_free": [{}, {}, {}],
        "ffn_last_pe": 0,
        "outsb_dma": [0, 0],
        "out_dma": [],
    }
    po_free_dve = [0, 0]

    C_LOADS = sync_dma(lambda s: s.dma_start(out=idn_f[:], in_=idn_d), q_c)
    for dst, src in ((maskD, maskD_d), (decQ, decQ_d), (decK, decK_d), (gCt, gC_d),
                     (gamb, gam_d), (betb, bet_d), (bgb, bgb_d), (bob, bob_d), (brb, brb_d)):
        C_LOADS = sync_dma(lambda s, dst=dst, src=src: s.dma_start(out=dst[:], in_=src), q_c)
    C_LOADS = sync_dma(lambda s: s.dma_start(
        out=b1stg[:].rearrange("hc (e p) -> hc e p", e=E),
        in_=b1_d.rearrange("e (hc p) -> hc e p", p=P)), q_c)
    C_LOADS = sync_dma(lambda s: s.dma_start(out=b2stg[:], in_=b2_d), q_c)

    CAST = {}
    wa_sems = {nm: Ctr(ctx.enter_context(nc.semaphore("q_" + nm))) for nm in _WA_NAMES}
    for _nm, _c in wa_sems.items():
        sems["q_" + _nm] = _c
    def cast_dma(name, out_ap, in_ap):
        prev = q_cast.n
        def f(g):
            g.wait_ge(q_cast.h, prev)
            g.dma_start(out=out_ap, in_=in_ap, max_dma_last_dim=8192).then_inc(q_cast.h, 16)
        prog.emit("gpsimd", f)
        CAST[name] = q_cast.inc(16)

    XP_PIECE = 4 * P
    ATTN_W = (("wv", wv_d, 2), ("wk", wk_d, 1), ("wg", wg_d, 3), ("wq", wq_d, 0), ("wo", wo_d, 4))
    def cast_attn_w(nm, src, slot):
        cast_dma(nm, wA_scr[slot * P:(slot + 1) * P, :].rearrange("p (kc n) -> p kc n", kc=KC),
                 src.rearrange("(kc p) n -> p kc n", p=P))
    n_xp = (NPREF * P + XP_PIECE - 1) // XP_PIECE
    _wa0 = {}
    def load_attn_w1(nm, slot):
        castc = CAST[nm]
        slot_off = {"wq": (WQ_OFF, regC), "wk": (WK_OFF, regC), "wv": (WV_OFF, regC),
                    "wg": (WG_OFF, regC), "wo": (WO_OFF, regD)}
        off, reg = slot_off[nm]
        sem = wa_sems[nm]
        def f(g, off=off, reg=reg, slot=slot, castc=castc, sem=sem):
            g.wait_ge(q_cast.h, castc)
            g.dma_start(out=reg[:, off:off + KC * D],
                        in_=wA_scr[slot * P:(slot + 1) * P, :]).then_inc(sem.h, 16)
        prog.emit("gpsimd", f)
        _wa0[nm] = (nm, sem.inc(16))
    cast_dma("xp0", xpbf_scr[0:min(XP_PIECE, NPREF * P), :], xp_d[0:min(XP_PIECE, NPREF * P), :])
    cast_attn_w(*ATTN_W[0]); load_attn_w1("wv", 2)
    cast_attn_w(*ATTN_W[1]); load_attn_w1("wk", 1)
    for i in range(1, n_xp):
        r0, r1 = i * XP_PIECE, min((i + 1) * XP_PIECE, NPREF * P)
        cast_dma(f"xp{i}", xpbf_scr[r0:r1, :], xp_d[r0:r1, :])
    cast_attn_w(*ATTN_W[2]); load_attn_w1("wg", 3)
    cast_attn_w(*ATTN_W[3]); load_attn_w1("wq", 0)
    cast_attn_w(*ATTN_W[4]); load_attn_w1("wo", 4)
    for i in range((NG * Tg + XP_PIECE - 1) // XP_PIECE):
        r0, r1 = i * XP_PIECE, min((i + 1) * XP_PIECE, NG * Tg)
        cast_dma(f"x{i}", xbf_scr[r0:r1, :], x_d[r0:r1, :])
    def cast_w1(e):
        for hh in range(2):
            cast_dma(f"w1_{e}" if hh == 1 else f"w1_{e}_a",
                     w1_scr[e * P:(e + 1) * P, :].rearrange("p (kc h) -> p kc h", kc=KC)[:, :, hh * 2048:(hh + 1) * 2048],
                     w1_d[e][:, hh * 2048:(hh + 1) * 2048].rearrange("(kc p) h -> p kc h", p=P))
    gp_dma(lambda g: g.dma_start(
        out=wrsb[:].rearrange("p (kc e) -> p kc e", kc=KC),
        in_=wr_d.rearrange("(kc p) e -> p kc e", p=P), max_dma_last_dim=32), q_wa)
    WR_READY = q_wa.n
    def cast_w2(e):
        cast_dma(f"w2_{e}", w2_scr[e * P:(e + 1) * P, :].rearrange("p (hc d) -> p hc d", hc=HC),
                 w2_d[e].rearrange("(hc p) d -> p hc d", p=P))
    cast_w1(0)
    cast_w2(0)


    def w_attn_loads(wait_pe):
        wa = {}
        slot_off = {"wq": (WQ_OFF, regC), "wk": (WK_OFF, regC), "wv": (WV_OFF, regC),
                    "wg": (WG_OFF, regC), "wo": (WO_OFF, regD)}
        for nm, src, slot in ATTN_W:
            off, reg = slot_off[nm]
            castc = CAST[nm]
            sem = wa_sems[nm]
            def f(g, nm=nm, off=off, reg=reg, slot=slot, castc=castc, wait_pe=wait_pe, sem=sem):
                g.wait_ge(q_cast.h, castc)
                if wait_pe:
                    g.wait_ge(pe.h, wait_pe)
                g.dma_start(out=reg[:, off:off + KC * D],
                            in_=wA_scr[slot * P:(slot + 1) * P, :]).then_inc(sem.h, 16)
            prog.emit("gpsimd", f)
            wa[nm] = (nm, sem.inc(16))
        st["wa"] = wa

    st["wa"] = dict(_wa0)

    eng_waits("vector", q_c=C_LOADS)
    vop(lambda v: v.memset(epsi[:], LN_EPS_IN))
    vop(lambda v: v.memset(eps5[:], LN_EPS))
    vop(lambda v: v.memset(S[:], 0.0))
    SBF0 = vop(lambda v: v.memset(Sbf[:], 0.0))
    st["sbf_ready"] = {"dve": SBF0}
    vop(lambda v: v.tensor_copy(idn_b[:], idn_f[:]))
    vop(lambda v: v.tensor_copy(b2hi[:], b2stg[:]))
    PREP = vop(lambda v: v.tensor_tensor(b2lo[:], b2stg[:], b2hi[:], op=ALU.subtract))

    def f(t, w=PREP):
        t.wait_ge(dve.h, w)
        last = None
        for e in range(E):
            last = t.transpose(pW[:, e * HC:(e + 1) * HC], b1stg[:, e * P:(e + 1) * P], idn_f[0:HC, 0:HC])
        last.then_inc(pe.h, 1)
    prog.emit("tensor", f)
    B1_PE = pe.inc()
    eng_waits("vector", pe=B1_PE)
    B1FM = vop(lambda v: v.tensor_copy(b1fm[:], pW[:, 0:E * HC]))
    st["pW_free"] = {"dve": B1FM}
    st["pc_free"] = {"dve": PREP}
    st["xtm_free_dve"] = [B1FM, B1FM]

    def dmax(*deps):
        out = {}
        for d in deps:
            if not d:
                continue
            if isinstance(d, int):
                d = {"dve": d}
            for k, v in d.items():
                out[k] = max(out.get(k, 0), v)
        return out

    def twd(t, deps):
        for k, v in deps.items():
            if v:
                t.wait_ge(sems[k].h, v)

    def ew(eng, deps):
        items = [(sems[k].h, v) for k, v in deps.items() if v > 0]
        if not items:
            return
        def f(e):
            for h, v in items:
                e.wait_ge(h, v)
        prog.emit(eng, f)

    def x_prep(xsrc_bf, castc, cloc, stage=None, extra_ld=None, slot_pe=0):
        stg = xy1bfc if stage is None else stage
        wpe = st["xbfc_free_pe"] if stage is None else 0
        eld = dict(extra_ld or {})
        qs = q_xc if stage is None else q_xe
        def fldr(s_, xsrc_bf=xsrc_bf, castc=castc, wpe=wpe, prev=qs.n, eld=eld, stg=stg, qs=qs):
            s_.wait_ge(qs.h, prev)
            s_.wait_ge(q_cast.h, castc)
            if wpe:
                s_.wait_ge(pe.h, wpe)
            for k, v in eld.items():
                s_.wait_ge(sems[k].h, v)
            s_.dma_start(out=stg[:, 0:D] if stage is not None else xy1bfc[:], in_=xsrc_bf).then_inc(qs.h, 16)
        prog.emit("sync", fldr)
        XC = qs.inc(16)
        if stage is not None:
            st["prep_q_xe"] = XC
        def f(t, w=XC, wb=dict(st["pc_free"]), stg=stg, stage=stage, qs=qs):
            t.wait_ge(qs.h, w)
            twd(t, wb)
            last = None
            src = stg[:, 0:D] if stage is not None else xy1bfc[:]
            for kc in range(KC):
                last = t.transpose(pC[:, kc * P:(kc + 1) * P], src[:, kc * P:(kc + 1) * P], idn_b[:])
            last.then_inc(pe.h, 1)
        prog.emit("tensor", f)
        XT_PE = pe.inc()
        if stage is None:
            st["xbfc_free_pe"] = XT_PE
        else:
            st["prep_xt_pe"] = XT_PE
        ew("scalar", {"pe": max(XT_PE, slot_pe)})
        ev = aop(lambda s, cloc=cloc: s.copy(
            regE[:, XF_OFF:XF_OFF + KC * 512].rearrange("p (kc t) -> p kc t", kc=KC)[:, :, cloc * P:(cloc + 1) * P],
            pC[:].rearrange("p (kc t) -> p kc t", kc=KC)))
        st["pc_free"] = {"act": ev}
        st["xfm_ready"] = {"act": ev}

    def v_proj(cloc):
        def f(t, wd=dmax(st["xfm_ready"], st["pB_free"]), wq=st["wa"]["wv"], cloc=cloc):
            twd(t, wd)
            t.wait_ge(sems["q_" + wq[0]].h, wq[1]) if isinstance(wq, tuple) else t.wait_ge(q_wa.h, wq)
            last = None
            for nh2 in range(2):
                for kc in range(KC):
                    last = t.matmul(pB[:, nh2 * 512:(nh2 + 1) * 512],
                                    regE[:, XF_OFF + kc * 512 + cloc * P: XF_OFF + kc * 512 + (cloc + 1) * P],
                                    regC[:, WV_OFF + kc * D + nh2 * 512: WV_OFF + kc * D + (nh2 + 1) * 512],
                                    start=(kc == 0), stop=(kc == KC - 1))
            last.then_inc(pe.h, 1)
        prog.emit("tensor", f)
        V_PE = pe.inc()
        ew("scalar", {"pe": V_PE})
        ev = aop(lambda s: s.copy(regE[:, VT_OFF:VT_OFF + D], pB[:]))
        st["vt_ready"] = {"act": ev}
        st["pB_free"] = {"act": ev}

    def sdelta_mm():
        def f(t, wd=dmax(st["kt_ready"], st["vt_ready"], st["pB_free"])):
            twd(t, wd)
            last = None
            for h in range(NH):
                pb_ = DH * (h % 2)
                col = (h // 2) * DH + 512 * (h % 2)
                dst = pB[pb_:pb_ + DH, col:col + DH]
                last = t.matmul(dst, regE[:, KT_OFF + h * DH:KT_OFF + (h + 1) * DH],
                                regE[:, VT_OFF + h * DH:VT_OFF + (h + 1) * DH],
                                start=(h < 2), stop=(h >= NH - 2), tile_position=(0, pb_))
            last.then_inc(pe.h, 1)
        prog.emit("tensor", f)
        st["sd_pe"] = pe.inc()

    def s_update():
        eng_waits("vector", pe=st["sd_pe"])
        vop(lambda v: v.tensor_tensor(S[:], S[:], gCt[:], op=ALU.mult))
        vop(lambda v: v.tensor_tensor(S[0:DH, 0:512], S[0:DH, 0:512], pB[0:DH, 0:512], op=ALU.add))
        SUP = vop(lambda v: v.tensor_tensor(S[DH:P, 0:512], S[DH:P, 0:512], pB[DH:P, 512:1024], op=ALU.add))
        st["pB_free"] = {"dve": SUP}
        ew("scalar", {"dve": SUP})
        st["sbf_ready"] = {"act": aop(lambda s: s.copy(Sbf[:], S[:]))}

    def sdelta_supdate():
        sdelta_mm()
        s_update()

    def prefix_chunk(pidx):
        if pidx == 0:
            x_prep(xpbf_scr[0:P, :], CAST["xp0"], 0)
        if pidx + 1 < NPREF:
            q = pidx + 1
            x_prep(xpbf_scr[q * P:(q + 1) * P, :], CAST[f"xp{q // 4}"], q % 2)
        v_proj(pidx % 2)
        def f(t, wd=dmax(st["xfm_ready"], st["pe_pf_free"][0], st["pe_pf_free"][1]), wq=st["wa"]["wk"],
              sl=pidx % 2):
            twd(t, wd)
            t.wait_ge(sems["q_" + wq[0]].h, wq[1]) if isinstance(wq, tuple) else t.wait_ge(q_wa.h, wq)
            last = None
            for nh2, bank in ((0, pE), (1, pF)):
                for kc in range(KC):
                    last = t.matmul(bank,
                                    regE[:, XF_OFF + kc * 512 + sl * P:XF_OFF + kc * 512 + (sl + 1) * P],
                                    regC[:, WK_OFF + kc * D + nh2 * 512: WK_OFF + kc * D + (nh2 + 1) * 512],
                                    start=(kc == 0), stop=(kc == KC - 1))
            last.then_inc(pe.h, 1)
        prog.emit("tensor", f)
        K_PE = pe.inc()
        eng_waits("vector", pe=K_PE)
        st["pe_pf_free"][0] = {"dve": vop(lambda v: v.tensor_tensor(
            regE[:, KT_OFF:KT_OFF + 512], pE, decK[:, 0:512], op=ALU.mult))}
        kt = vop(lambda v: v.tensor_tensor(
            regE[:, KT_OFF + 512:KT_OFF + D], pF, decK[:, 512:D], op=ALU.mult))
        st["pe_pf_free"][1] = {"dve": kt}
        st["kt_ready"] = {"dve": kt}
        sdelta_supdate()

    def qk_proj():
        last_ev = 0
        for woff, foff in ((WQ_OFF, QF_OFF), (WK_OFF, KF_OFF)):
            wkey = "wq" if woff == WQ_OFF else "wk"
            for nt in range(KC):
                bidx = nt % 2
                bank = pE if bidx == 0 else pF
                def f(t, woff=woff, nt=nt, bank=bank,
                      wd=dmax(st["pe_pf_free"][bidx], st["xfm_ready"]), wq=st["wa"][wkey]):
                    twd(t, wd)
                    t.wait_ge(sems["q_" + wq[0]].h, wq[1]) if isinstance(wq, tuple) else t.wait_ge(q_wa.h, wq)
                    last = None
                    for kc in range(KC):
                        last = t.matmul(bank,
                                        regC[:, woff + kc * D + nt * P: woff + kc * D + (nt + 1) * P],
                                        regE[:, XF_OFF + kc * 512:XF_OFF + (kc + 1) * 512],
                                        start=(kc == 0), stop=(kc == KC - 1))
                    last.then_inc(pe.h, 1)
                prog.emit("tensor", f)
                QK_PE = pe.inc()
                ew("scalar", {"pe": QK_PE})
                last_ev = aop(lambda s, foff=foff, nt=nt, bank=bank: s.copy(
                    regD[:, foff + nt * 512:foff + (nt + 1) * 512], bank[:]))
                st["pe_pf_free"][bidx] = {"act": last_ev}
        st["qk_ready"] = {"act": last_ev}

    ebuf = stbf[0]
    ebuf_full = regG[:, D:2 * D]

    def front_early(g, tb, cloc):
        tc = tb * CPT + cloc
        xb = tc % 2
        ow = {("q_o0" if xb == 0 else "q_o1"): st["outsb_dma"][xb]}
        eng_waits("sync", dve=st["xtm_free_dve"][xb], **ow)
        st["xd"][xb] = sync_dma(lambda s, g=g, tc=tc, xb=xb: s.dma_start(
            out=xtm[xb][:], in_=x_d[(g * Tg + tc * P):(g * Tg + (tc + 1) * P), :]), q_x[xb])
        eng_waits("vector", **{("q_x0" if xb == 0 else "q_x1"): st["xd"][xb]})
        st["xbob"] = {"dve": vop(lambda v, xb=xb: v.tensor_tensor(
            xtm[xb][:], xtm[xb][:], bob[:], op=ALU.add))}

        v_proj(cloc)

        def f(t, wd=dmax(st["xfm_ready"], st["pB_free"]), cloc=cloc, wq=st["wa"]["wg"]):
            twd(t, wd)
            t.wait_ge(sems["q_" + wq[0]].h, wq[1]) if isinstance(wq, tuple) else t.wait_ge(q_wa.h, wq)
            last = None
            for nh2 in range(2):
                for kc in range(KC):
                    last = t.matmul(pB[:, nh2 * 512:(nh2 + 1) * 512],
                                    regE[:, XF_OFF + kc * 512 + cloc * P: XF_OFF + kc * 512 + (cloc + 1) * P],
                                    regC[:, WG_OFF + kc * D + nh2 * 512: WG_OFF + kc * D + (nh2 + 1) * 512],
                                    start=(kc == 0), stop=(kc == KC - 1))
            last.then_inc(pe.h, 1)
        prog.emit("tensor", f)
        G_PE = pe.inc()
        eng_waits("vector", pe=G_PE)
        vop(lambda v: v.tensor_tensor(pB[:, 0:512], pB[:, 0:512], bgb[:, 0:512], op=ALU.add))
        Z1 = vop(lambda v: v.tensor_tensor(pB[:, 512:1024], pB[:, 512:1024], bgb[:, 512:1024], op=ALU.add))
        eng_waits("scalar", dve=max(Z1, st["ea_ebuf_dve"]))
        SIL = aop(lambda s: s.activation(lnG[:], pB[:], AF.Silu))
        eng_waits("vector", act=SIL)
        GG = vop(lambda v: v.tensor_tensor(regE[:, GT_OFF:GT_OFF + D], lnG[:], gamb[:], op=ALU.mult))
        BG = vop(lambda v: v.tensor_tensor(lnG[:], lnG[:], betb[:], op=ALU.mult))
        st["pB_free"] = {"act": SIL}
        st["gG_ready"] = {"dve": GG}
        st["bG_ready"] = {"dve": BG}

        def f(t, wd=dmax(st["pc_free"], st["qk_ready"]), cloc=cloc):
            twd(t, wd)
            last = None
            for nt in range(KC):
                last = t.transpose(pC[:, nt * P:(nt + 1) * P],
                                   regD[:, KF_OFF + nt * 512 + cloc * P: KF_OFF + nt * 512 + (cloc + 1) * P],
                                   idn_b[:])
            last.then_inc(pe.h, 1)
        prog.emit("tensor", f)
        KT_PE = pe.inc()
        eng_waits("vector", pe=KT_PE)
        kt = vop(lambda v: v.tensor_tensor(
            regE[:, KT_OFF:KT_OFF + D], pC[:], decK[:], op=ALU.mult))
        st["pc_free"] = {"dve": kt}
        st["kt_ready"] = {"dve": kt}

        st["qd_ready"] = {"dve": vop(lambda v, cloc=cloc: v.tensor_tensor(
            regE[:, QD_OFF:QD_OFF + D],
            regD[:, QF_OFF:QF_OFF + KC * 512].rearrange("p (nt t) -> p nt t", nt=KC)[:, :, cloc * P:(cloc + 1) * P],
            decQ[:].rearrange("p (nt t) -> p nt t", nt=KC), op=ALU.mult))}

        sdelta_mm()

    def front_late(g, tb, cloc):
        ret_started = [False, False]
        last_ret_pe = 0
        for b4 in range(4):
            bidx = b4 % 2
            def f(t, b4=b4, wd=dmax(st["pe_pf_free"][0], st["pe_pf_free"][1], st["qk_ready"]), cloc=cloc):
                twd(t, wd)
                last = None
                for j in range(4):
                    h = b4 * 4 + j
                    nt = h // 2
                    pbase = (h % 2) * DH
                    bank = pE if j % 2 == 0 else pF
                    col = (j // 2) * P
                    lhsT = regD[pbase:pbase + DH, KF_OFF + nt * 512 + cloc * P: KF_OFF + nt * 512 + (cloc + 1) * P]
                    rhs = regD[pbase:pbase + DH, QF_OFF + nt * 512 + cloc * P: QF_OFF + nt * 512 + (cloc + 1) * P]
                    last = t.matmul(bank[:, col:col + P], lhsT, rhs, start=True, stop=True)
                last.then_inc(pe.h, 1)
            prog.emit("tensor", f)
            SC_PE = pe.inc()
            eng_waits("vector", pe=SC_PE)
            src = bass.AP(pEF, 0, [[pEF.shape[1], P], [512, 2], [P, 2], [1, P]])
            dst = bass.AP(regG, D + bidx * 512, [[3 * D, P], [P, 2], [2 * P, 2], [1, P]])
            msk = bass.AP(maskD, b4 * 4 * P, [[NH * P, P], [P, 2], [2 * P, 2], [1, P]])
            ST_DVE = vop(lambda v, src=src, dst=dst, msk=msk: v.tensor_tensor(dst, src, msk, op=ALU.mult))
            st["pe_pf_free"][0] = {"dve": ST_DVE}
            st["pe_pf_free"][1] = {"dve": ST_DVE}
            s0, s1 = ret_started
            def f(t, b4=b4, bidx=bidx,
                  wd=dmax({"dve": ST_DVE}, st["vt_ready"], st["qd_ready"], st["sbf_ready"], st["pA_free"]),
                  s0=s0, s1=s1):
                twd(t, wd)
                started = [s0, s1]
                last = None
                for j in range(4):
                    h = b4 * 4 + j
                    bk = 0 if h < 8 else 1
                    dst = pA[:, h * DH:(h + 1) * DH]
                    last = t.matmul(dst, stbf[bidx][:, j * P:(j + 1) * P],
                                    regE[:, VT_OFF + h * DH:VT_OFF + (h + 1) * DH],
                                    start=(not started[bk]), stop=False)
                    started[bk] = True
                    pbase = (h % 2) * DH
                    qd = regE[pbase:pbase + DH, QD_OFF + (h // 2) * P: QD_OFF + (h // 2 + 1) * P]
                    sbf_sl = Sbf[pbase:pbase + DH, (h // 2) * DH:(h // 2 + 1) * DH]
                    last = t.matmul(dst, qd, sbf_sl, start=False, stop=(h in (7, 15)))
                last.then_inc(pe.h, 1)
            prog.emit("tensor", f)
            last_ret_pe = pe.inc()
            if b4 < 2:
                ret_started[0] = True
            else:
                ret_started[1] = True
            if b4 == 1:
                eng_waits("vector", pe=last_ret_pe)
                vop(lambda v: v.bn_stats(bnbuf[:, 0:6], pA[:, 0:512]))
        st["st_free_pe_last"] = last_ret_pe
        eng_waits("vector", pe=last_ret_pe)
        vop(lambda v: v.bn_stats(bnbuf[:, 6:12], pA[:, 512:1024]))
        s_update()

    def epi_a(g, tb, cloc):
        BN = vop(lambda v: v.bn_aggr(mvar[:], bnbuf[:].rearrange("p (n t) -> p n t", t=3)))
        eng_waits("scalar", dve=BN)
        aop(lambda s: s.activation(lnv[:], mvar[:, 1:2], AF.Ln, bias=epsi[:]))
        RS = aop(lambda s: s.activation(rstd[:], lnv[:], AF.Exp, scale=-0.5))
        eng_waits("vector", act=RS, pe=st["prep_xt_pe"], q_xe=st["prep_q_xe"])
        TS1 = vop(lambda v: v.tensor_scalar(ebuf_full, pA[:], mvar[:, 0:1], rstd[:], op0=ALU.subtract, op1=ALU.mult))
        st["pA_free"] = {"dve": TS1}
        vop(lambda v: v.tensor_tensor(ebuf_full, ebuf_full, regE[:, GT_OFF:GT_OFF + D], op=ALU.mult))
        EAE = vop(lambda v: v.tensor_tensor(xy1bfc[:], ebuf_full, lnG[:], op=ALU.add))
        st["lnG_ready"] = {"dve": EAE}
        st["ea_ebuf_dve"] = EAE

    def epi_b(g, tb, cloc):
        tc = tb * CPT + cloc
        xb = tc % 2
        def f(t, wd=dmax(st["lnG_ready"], st["pc_free"], {"pe": st["xbfc_free_pe"]})):
            twd(t, wd)
            last = None
            for kc in range(KC):
                last = t.transpose(pC[:, kc * P:(kc + 1) * P], xy1bfc[:, kc * P:(kc + 1) * P], idn_b[:])
            last.then_inc(pe.h, 1)
        prog.emit("tensor", f)
        LT_PE = pe.inc()
        st["xbfc_free_pe"] = LT_PE
        ew("scalar", {"pe": LT_PE})
        LNGF = aop(lambda s, cloc=cloc: s.copy(
            regE[:, XF_OFF:XF_OFF + KC * 512].rearrange("p (kc t) -> p kc t", kc=KC)[:, :, cloc * P:(cloc + 1) * P],
            pC[:].rearrange("p (kc t) -> p kc t", kc=KC)))
        st["pc_free"] = {"act": LNGF}

        def f(t, wd=dmax({"act": LNGF}, st["pe_pf_free"][0], st["pe_pf_free"][1]), wq=st["wa"]["wo"], cloc=cloc):
            twd(t, wd)
            t.wait_ge(sems["q_" + wq[0]].h, wq[1]) if isinstance(wq, tuple) else t.wait_ge(q_wa.h, wq)
            last = None
            for nh2, bank in ((0, pE), (1, pF)):
                for kc in range(KC):
                    last = t.matmul(bank,
                                    regE[:, XF_OFF + kc * 512 + cloc * P: XF_OFF + kc * 512 + (cloc + 1) * P],
                                    regD[:, WO_OFF + kc * D + nh2 * 512: WO_OFF + kc * D + (nh2 + 1) * 512],
                                    start=(kc == 0), stop=(kc == KC - 1))
            last.then_inc(pe.h, 1)
        prog.emit("tensor", f)
        AO_PE = pe.inc()
        st["ao_pe"] = AO_PE

        eng_waits("vector", pe=AO_PE)
        st["pe_pf_free"][0] = {"dve": vop(lambda v, xb=xb: v.tensor_tensor(
            t1r1[:, 0:512], pE, xtm[xb][:, 0:512], op=ALU.add))}
        R1X = vop(lambda v, xb=xb: v.tensor_tensor(
            t1r1[:, 512:1024], pF, xtm[xb][:, 512:1024], op=ALU.add))
        st["pe_pf_free"][1] = {"dve": R1X}
        st["xtm_free_dve"][xb] = R1X
        vop(lambda v: v.bn_stats(bnbuf[:, 0:6], t1r1[:, 0:512]))
        vop(lambda v: v.bn_stats(bnbuf[:, 6:12], t1r1[:, 512:1024]))
        BN2 = vop(lambda v: v.bn_aggr(mvar[:], bnbuf[:].rearrange("p (n t) -> p n t", t=3)))
        eng_waits("scalar", dve=BN2)
        aop(lambda s: s.activation(lnv[:], mvar[:, 1:2], AF.Ln, bias=eps5[:]))
        RS2 = aop(lambda s: s.activation(rstd[:], lnv[:], AF.Exp, scale=-0.5))
        eng_waits("vector", act=RS2)
        ACC_W = vop(lambda v, tc=tc: v.tensor_scalar(acc[:, tc * D:(tc + 1) * D], t1r1[:],
                                                     mvar[:, 0:1], rstd[:], op0=ALU.subtract, op1=ALU.mult))

        ew("scalar", {"pe": st["xbfc_free_pe"], "dve": ACC_W})
        Y1C = aop(lambda s, tc=tc: s.copy(xy1bfc[:], acc[:, tc * D:(tc + 1) * D]))
        def f(t, wd=dmax({"act": Y1C}, st["pc_free"])):
            twd(t, wd)
            last = None
            for kc in range(KC):
                last = t.transpose(pC[:, kc * P:(kc + 1) * P], xy1bfc[:, kc * P:(kc + 1) * P], idn_b[:])
            last.then_inc(pe.h, 1)
        prog.emit("tensor", f)
        YT_PE = pe.inc()
        st["xbfc_free_pe"] = YT_PE
        ew("scalar", {"pe": YT_PE})
        Y1F = aop(lambda s, tc=tc: s.copy(
            y1fm[:].rearrange("p (kc t) -> p kc t", kc=KC)[:, :, tc * P:(tc + 1) * P],
            pC[:].rearrange("p (kc t) -> p kc t", kc=KC)))
        st["pc_free"] = {"act": Y1F}
        st["y1fm_ready"] = {"act": Y1F}

        def f(t, wd=dmax({"act": Y1F}, st["pW_free"]), wq=WR_READY, tc=tc, cloc=cloc):
            twd(t, wd)
            t.wait_ge(sems["q_" + wq[0]].h, wq[1]) if isinstance(wq, tuple) else t.wait_ge(q_wa.h, wq)
            last = None
            for kc in range(KC):
                last = t.matmul(pW[:, cloc * E:(cloc + 1) * E],
                                y1fm[:].rearrange("p (kc t) -> p kc t", kc=KC)[:, kc, tc * P:(tc + 1) * P],
                                wrsb[:, kc * E:(kc + 1) * E], start=(kc == 0), stop=(kc == KC - 1))
            last.then_inc(pe.h, 1)
        prog.emit("tensor", f)
        st["rl_pe"] = pe.inc()

    def router_tblock(g, tb):
        v3 = lambda ap: ap.rearrange("p (c e) -> p c e", c=CPT)
        bc = lambda t4: bass.AP(t4, 0, [[CPT, P], [1, CPT], [0, E]])
        eng_waits("vector", pe=st["rl_pe"])
        vop(lambda v: v.tensor_tensor(
            rl[:].rearrange("p (c e) -> p c e", c=CPT),
            pW[:, 0:CPT * E].rearrange("p (c e) -> p c e", c=CPT),
            bass.AP(brb, 0, [[E, P], [0, CPT], [1, E]]), op=ALU.add))
        RL_D = vop(lambda v: v.tensor_reduce(rm1[:], v3(rl[:]), op=ALU.max, axis=AX.X))
        st["pW_free"] = {"dve": RL_D}
        vop(lambda v: v.tensor_tensor(req[:].rearrange("p (c e) -> p c e", c=CPT),
                                      v3(rl[:]), bc(rm1), op=ALU.is_equal))
        vop(lambda v: v.scalar_tensor_tensor(rl2[:], req[:], -1e30, rl[:], op0=ALU.mult, op1=ALU.add))
        vop(lambda v: v.tensor_reduce(rm2[:], v3(rl2[:]), op=ALU.max, axis=AX.X))
        vop(lambda v: v.tensor_tensor(rge[:].rearrange("p (c e) -> p c e", c=CPT),
                                      v3(rl[:]), bc(rm2), op=ALU.is_ge))
        SUB = vop(lambda v: v.tensor_tensor(rl2[:].rearrange("p (c e) -> p c e", c=CPT),
                                            v3(rl[:]), bc(rm1), op=ALU.subtract))
        eng_waits("scalar", dve=SUB)
        REXP = aop(lambda s: s.activation(rexp[:], rl2[:], AF.Exp))
        eng_waits("vector", act=REXP)
        vop(lambda v: v.tensor_tensor(rexp[:], rexp[:], rge[:], op=ALU.mult))
        vop(lambda v: v.tensor_reduce(rsum[:], v3(rexp[:]), op=ALU.add, axis=AX.X))
        vop(lambda v: v.reciprocal(rrs[:], rsum[:]))
        WSB = vop(lambda v, tb=tb: v.tensor_tensor(
            wsb[:, tb * CPT * E:(tb + 1) * CPT * E].rearrange("p (c e) -> p c e", c=CPT),
            v3(rexp[:]), bc(rrs), op=ALU.mult))
        return WSB


    w1_ready = {}
    def w1_load(key, e, half, slab, extra_pe=0):
        buf = slab % 2
        wait_pe = max(st["w1_free_pe"][buf], extra_pe)
        sem = q_w1[buf]
        castc = CAST[f"w1_{e}"]
        sg = half * N_SLAB + slab
        def f(s_, e=e, sg=sg, buf=buf, wait_pe=wait_pe, sem=sem, castc=castc):
            s_.wait_ge(q_cast.h, castc)
            if wait_pe:
                s_.wait_ge(pe.h, wait_pe)
            s_.dma_start(out=w1buf[buf].rearrange("p (kc h) -> p kc h", kc=KC),
                         in_=w1_scr[e * P:(e + 1) * P, :].rearrange(
                             "p (kc h) -> p kc h", kc=KC)[:, :, sg * SLAB_H:(sg + 1) * SLAB_H]
                         ).then_inc(sem.h, 16)
        prog.emit("sync", f)
        w1_ready[key] = (buf, sem.inc(16))

    w2_ready = {}
    def w2_load(key, e, wait_pe):
        castc = CAST[f"w2_{e}"]
        def f(g_, e=e, wait_pe=wait_pe, castc=castc):
            g_.wait_ge(q_cast.h, castc)
            if wait_pe:
                g_.wait_ge(pe.h, wait_pe)
            g_.dma_start(out=w2buf[:], in_=w2_scr[e * P:(e + 1) * P, :]).then_inc(q_w2.h, 16)
        prog.emit("gpsimd", f)
        w2_ready[key] = q_w2.inc(16)

    def ffn_group(g, wsb_ready, acc_ready_act, y1fm_ready):
        WF_EV = 0
        for rnd in range(2):
            def f(t, wd=dmax({"dve": max(wsb_ready, WF_EV)}, st["pW_free"], {"pe": st["wfm_reader_pe"]}), rnd=rnd):
                twd(t, wd)
                last = None
                for i in range(4):
                    tc = rnd * 4 + i
                    last = t.transpose(pW[0:E, i * P:(i + 1) * P], wsb[:, tc * E:(tc + 1) * E], idn_f[:])
                last.then_inc(pe.h, 1)
            prog.emit("tensor", f)
            WT_PE = pe.inc()
            eng_waits("vector", pe=WT_PE)
            vop(lambda v, rnd=rnd: v.tensor_copy(wfmhi[0:E, rnd * 512:(rnd + 1) * 512], pW[0:E, 0:512]))
            WF_EV = vop(lambda v, rnd=rnd: v.tensor_tensor(
                wfmlo[0:E, rnd * 512:(rnd + 1) * 512], pW[0:E, 0:512],
                wfmhi[0:E, rnd * 512:(rnd + 1) * 512], op=ALU.subtract))
        st["pW_free"] = {"dve": WF_EV}
        st["ph_free"][2] = dmax(st["ph_free"][2], {"dve": WF_EV})
        WFM_READY = WF_EV

        po_free = [dmax({"dve": po_free_dve[0]}, st["pA_free"]),
                   dmax({"dve": po_free_dve[1]}, st["pB_free"])]
        last_b2_pe = 0
        for tc in range(TC):
            pb = tc % 2
            bank = pA if pb == 0 else pB
            def f(t, tc=tc, bank=bank, wd=dmax({"dve": max(WFM_READY, PREP)}, po_free[pb], st["pA_free"])):
                twd(t, wd)
                last = None
                for nh in range(2):
                    dst = bank[:, nh * 512:(nh + 1) * 512]
                    for i, (lhs, rhs) in enumerate(((wfmhi, b2hi), (wfmhi, b2lo), (wfmlo, b2hi))):
                        last = t.matmul(dst, lhs[0:E, tc * P:(tc + 1) * P],
                                        rhs[0:E, nh * 512:(nh + 1) * 512],
                                        start=(i == 0), stop=(i == 2))
                last.then_inc(pe.h, 1)
            prog.emit("tensor", f)
            last_b2_pe = pe.inc()
            eng_waits("vector", pe=last_b2_pe, act=acc_ready_act)
            po_free_dve[pb] = vop(lambda v, tc=tc, bank=bank: v.tensor_tensor(
                acc[:, tc * D:(tc + 1) * D], acc[:, tc * D:(tc + 1) * D], bank[:], op=ALU.add))
            po_free[pb] = {"dve": po_free_dve[pb]}
        st["wfm_reader_pe"] = last_b2_pe

        pepf0 = [dict(st["pe_pf_free"][0]), dict(st["pe_pf_free"][1])]
        for e in range(E):
            if g == 0 and e + 1 < E:
                cast_w1(e + 1)
                cast_w2(e + 1)
            for half in range(NHALF):
                evac_start = act.n
                gelu_cnt = [0] * HCH
                for hcl in range(HCH):
                    slab = hcl // HC_PER_SLAB
                    key = (g, e, half, slab)
                    if key not in w1_ready:
                        w1_load(key, e, half, slab)
                    for th in range(2):
                        it = hcl * 2 + th
                        b = (half * HCH * 2 + it) % 3
                        bank = (pE, pF, pW[:])[b]
                        wqbuf, wqcnt = w1_ready[key]
                        def f(t, e=e, half=half, hcl=hcl, th=th, b=b, bank=bank,
                              wqbuf=wqbuf, wqcnt=wqcnt,
                              wd=dmax(st["ph_free"][b], y1fm_ready, pepf0[b % 2])):
                            t.wait_ge(q_w1[wqbuf].h, wqcnt)
                            twd(t, wd)
                            last = None
                            hloc = hcl % HC_PER_SLAB
                            wb_ = w1buf[(hcl // HC_PER_SLAB) % 2]
                            for kc in range(KC):
                                lhsT = wb_[:, kc * SLAB_H + hloc * P: kc * SLAB_H + (hloc + 1) * P]
                                rhs = y1fm[:, kc * Tg + th * 512: kc * Tg + (th + 1) * 512]
                                last = t.matmul(bank, lhsT, rhs, start=(kc == 0), stop=(kc == KC - 1))
                            last.then_inc(pe.h, 1)
                        prog.emit("tensor", f)
                        mm_pe = pe.inc()
                        if hcl % HC_PER_SLAB == HC_PER_SLAB - 1 and th == 1:
                            st["w1_free_pe"][(hcl // HC_PER_SLAB) % 2] = mm_pe
                            nslab = slab + 2
                            if nslab < N_SLAB:
                                w1_load((g, e, half, nslab), e, half, nslab)
                            elif half + 1 < NHALF:
                                w1_load((g, e, half + 1, nslab - N_SLAB), e, half + 1, nslab - N_SLAB)
                            elif e + 1 < E:
                                w1_load((g, e + 1, 0, nslab - N_SLAB), e + 1, 0, nslab - N_SLAB)
                        eng_waits("scalar", pe=mm_pe)
                        gc = aop(lambda s, e=e, half=half, hcl=hcl, th=th, bank=bank: s.activation(
                            hbuf[:, hcl * Tg + th * 512: hcl * Tg + (th + 1) * 512],
                            bank, AF.Gelu,
                            bias=b1fm[:, e * HC + half * HCH + hcl: e * HC + half * HCH + hcl + 1]))
                        st["ph_free"][b] = {"act": gc}
                        gelu_cnt[hcl] = gc
                evac_end = act.n
                for tc in range(TC):
                    pb = tc % 2
                    bank = pA if pb == 0 else pB
                    def f(t, e=e, half=half, tc=tc, bank=bank,
                          wact=evac_end, ww2=w2_ready[(g, e)], wd=dict(po_free[pb])):
                        t.wait_ge(act.h, wact)
                        t.wait_ge(q_w2.h, ww2)
                        twd(t, wd)
                        last = None
                        for nh in range(2):
                            dst = bank[:, nh * 512:(nh + 1) * 512]
                            for hcl in range(HCH):
                                hc = half * HCH + hcl
                                lhsT = hbuf[:, hcl * Tg + tc * P: hcl * Tg + (tc + 1) * P]
                                rhs = w2buf[:, hc * D + nh * 512: hc * D + (nh + 1) * 512]
                                last = t.matmul(dst, lhsT, rhs, start=(hcl == 0), stop=(hcl == HCH - 1))
                        last.then_inc(pe.h, 1)
                    prog.emit("tensor", f)
                    mm_pe = pe.inc()
                    if tc == TC - 1 and half == NHALF - 1:
                        st["w2_free_pe"] = mm_pe
                    eng_waits("vector", pe=mm_pe)
                    po_free_dve[pb] = vop(lambda v, e=e, tc=tc, bank=bank: v.scalar_tensor_tensor(
                        acc[:, tc * D:(tc + 1) * D], bank[:], wsb[:, tc * E + e: tc * E + e + 1],
                        acc[:, tc * D:(tc + 1) * D], op0=ALU.mult, op1=ALU.add))
                    po_free[pb] = {"dve": po_free_dve[pb]}
            if e + 1 < E:
                w2_load((g, e + 1), e + 1, st["w2_free_pe"])
        st["ffn_last_pe"] = pe.n
        st["pA_free"] = {"dve": po_free_dve[0]}
        st["pB_free"] = {"dve": po_free_dve[1]}
        st["pW_free"] = dmax(st["pW_free"], st["ph_free"][2])
        st["xbfc_free_pe"] = max(st["xbfc_free_pe"], st["wfm_reader_pe"])

        for tc in range(TC):
            vop(lambda v, tc=tc: v.bn_stats(bnbuf[:, 0:6], acc[:, tc * D: tc * D + 512]))
            vop(lambda v, tc=tc: v.bn_stats(bnbuf[:, 6:12], acc[:, tc * D + 512: tc * D + 1024]))
            BN3 = vop(lambda v: v.bn_aggr(mvar[:], bnbuf[:].rearrange("p (n t) -> p n t", t=3)))
            eng_waits("scalar", dve=BN3)
            aop(lambda s: s.activation(lnv[:], mvar[:, 1:2], AF.Ln, bias=eps5[:]))
            RS3 = aop(lambda s: s.activation(rstd[:], lnv[:], AF.Exp, scale=-0.5))
            eng_waits("vector", act=RS3)
            ob = tc % 2
            prev = st["outsb_dma"][ob]
            eng_waits("vector", **{("q_o0" if ob == 0 else "q_o1"): prev})
            OA = vop(lambda v, tc=tc, ob=ob: v.tensor_scalar(outsb[ob][:], acc[:, tc * D:(tc + 1) * D],
                                                             mvar[:, 0:1], rstd[:], op0=ALU.subtract, op1=ALU.mult))
            def f(s_, g=g, tc=tc, ob=ob, OA=OA):
                s_.wait_ge(dve.h, OA)
                s_.dma_start(out=y_d[g * Tg + tc * P: g * Tg + (tc + 1) * P, :],
                             in_=outsb[ob][:]).then_inc(q_o[ob].h, 16)
            prog.emit("sync", f)
            st["outsb_dma"][ob] = q_o[ob].inc(16)

    for pidx in range(NPREF):
        prefix_chunk(pidx)

    if DEBUG_STAGE == "prefix":
        SS = st["sbf_ready"].get("dve", 0)
        SSA = st["sbf_ready"].get("act", 0)
        def fdbg(s_, SS=SS, SSA=SSA):
            if SS:
                s_.wait_ge(dve.h, SS)
            if SSA:
                s_.wait_ge(act.h, SSA)
            s_.dma_start(out=y_d[0:P, 0:512], in_=S[:]).then_inc(q_o0.h, 16)
            s_.wait_ge(q_o0.h, 16)
        prog.emit("sync", fdbg)
        _assemble(nc, prog, ctx)
        return nc

    for g in range(NG):
        wsb_last = 0
        for tb in range(NTB):
            if tb == 0:
                for cloc in range(CPT):
                    tc = cloc
                    gtc = g * TC + tc
                    x_prep(xbf_scr[(g * Tg + tc * P):(g * Tg + (tc + 1) * P), :], CAST[f"x{gtc // 4}"], cloc)
            qk_proj()
            front_early(g, tb, 0)
            for cloc in range(CPT):
                front_late(g, tb, cloc)
                epi_a(g, tb, cloc)
                if cloc + 1 < CPT:
                    front_early(g, tb, cloc + 1)
                epi_b(g, tb, cloc)
                if tb == 0:
                    tc1 = CPT + cloc
                    gtc1 = g * TC + tc1
                    x_prep(xbf_scr[(g * Tg + tc1 * P):(g * Tg + (tc1 + 1) * P), :], CAST[f"x{gtc1 // 4}"],
                           cloc, stage=ebuf_full,
                           extra_ld={"dve": st["ea_ebuf_dve"], "pe": st["st_free_pe_last"]},
                           slot_pe=st["ao_pe"])
            wsb_last = router_tblock(g, tb)
        attn_end_pe = pe.n
        if DEBUG_STAGE == "attn":
            AC = act.n
            WS = dve.n
            for tc in range(TC):
                def fdbg(s_, g=g, tc=tc, AC=AC, WS=WS):
                    s_.wait_ge(act.h, AC)
                    s_.wait_ge(dve.h, WS)
                    s_.dma_start(out=y_d[g * Tg + tc * P: g * Tg + (tc + 1) * P, :],
                                 in_=acc[:, tc * D:(tc + 1) * D]).then_inc(q_o0.h, 16)
                prog.emit("sync", fdbg)
                q_o0.inc(16)
            st["pA_free"] = dmax(st["pA_free"], {"dve": dve.n})
            st["pB_free"] = dmax(st["pB_free"], {"dve": dve.n})
            continue
        w2_load((g, 0), 0, attn_end_pe)
        w1_load((g, 0, 0, 0), 0, 0, 0, extra_pe=attn_end_pe)
        w1_load((g, 0, 0, 1), 0, 0, 1, extra_pe=attn_end_pe)
        ffn_group(g, wsb_last, act.n, dict(st["y1fm_ready"]))
        if g + 1 < NG:
            w_attn_loads(st["ffn_last_pe"])

    def fin(s_):
        s_.wait_ge(q_o0.h, q_o0.n)
        s_.wait_ge(q_o1.h, q_o1.n)
    prog.emit("sync", fin)

    _assemble(nc, prog, ctx)
    return nc


def _assemble(nc, prog, ctx):
    with nc.Block() as block:
        @block.sync
        def _(s_):
            for fn in prog.ops["sync"]:
                fn(s_)
        @block.gpsimd
        def _(g_):
            for fn in prog.ops["gpsimd"]:
                fn(g_)
        @block.tensor
        def _(t_):
            for fn in prog.ops["tensor"]:
                fn(t_)
        @block.vector
        def _(v_):
            for fn in prog.ops["vector"]:
                fn(v_)
        @block.scalar
        def _(s_):
            for fn in prog.ops["scalar"]:
                fn(s_)
    ctx.close()


def kernel(**inputs):
    x = np.ascontiguousarray(np.asarray(inputs["L0_attn_x"], dtype=np.float32))
    consts = host_consts()
    zeros_pref = np.zeros((NPREF * P, D), np.float32)
    base = {
        "Wq": inputs["L0_attn_Wq"], "Wk": inputs["L0_attn_Wk"], "Wv": inputs["L0_attn_Wv"],
        "Wg": inputs["L0_attn_Wg"], "Wo": inputs["L0_attn_Wo"],
        "W1": inputs["L0_ffn_W1"], "b1": inputs["L0_ffn_b1"],
        "W2": inputs["L0_ffn_W2"], "b2": inputs["L0_ffn_b2"],
        "Wr": inputs["L0_ffn_Wr"],
    }
    base = {k: np.ascontiguousarray(np.asarray(v, dtype=np.float32)) for k, v in base.items()}
    bcast = lambda v, n: np.ascontiguousarray(np.broadcast_to(np.asarray(v, np.float32), (P, n)))
    base["bg_b"] = bcast(inputs["L0_attn_bg"], D).astype(ml_dtypes.bfloat16)
    base["bo_b"] = bcast(inputs["L0_attn_bo"], D).astype(ml_dtypes.bfloat16)
    base["gam_b"] = bcast(inputs["L0_attn_ln_gamma"], D).astype(ml_dtypes.bfloat16)
    base["bet_b"] = bcast(inputs["L0_attn_ln_beta"], D).astype(ml_dtypes.bfloat16)
    base["br_b"] = bcast(inputs["L0_ffn_br"], E)
    base.update(consts)

    in_maps = []
    for c in range(8):
        b, half = c // 2, c % 2
        m = dict(base)
        m["x"] = np.ascontiguousarray(x[b, half * 4096:(half + 1) * 4096])
        m["xp"] = np.ascontiguousarray(x[b, 0:4096]) if half == 1 else zeros_pref
        in_maps.append(m)

    nc = build_program()
    res = run_bass_kernel_spmd(nc, in_maps, core_ids=list(range(8)))
    out = np.empty((4, 8192, 1024), np.float32)
    for c in range(8):
        b, half = c // 2, c % 2
        out[b, half * 4096:(half + 1) * 4096] = res.results[c]["y"]
    return out


# version 23
# speedup vs baseline: 1.1436x; 1.0371x over previous
import numpy as np
import ml_dtypes
import concourse.bass as bass
import concourse.mybir as mybir
from concourse.bass_utils import run_bass_kernel_spmd
from contextlib import ExitStack

F32 = mybir.dt.float32
BF16 = mybir.dt.bfloat16
AF = mybir.ActivationFunctionType
ALU = mybir.AluOpType
AX = mybir.AxisListType

P = 128
D = 1024
H = 4096
E = 8
NH = 16
DH = 64
KC = D // P
HC = H // P
Tg = 1024
TC = Tg // P
NTB = 2
CPT = 4
NG = 4
NHALF = 2
HCH = HC // NHALF
SLAB_H = 512
N_SLAB = (HCH * P) // SLAB_H
HC_PER_SLAB = SLAB_H // P
NPREF = 32
LN_EPS_IN = 1e-3
LN_EPS = 1e-5
DEBUG_STAGE = None


class Ctr:
    def __init__(self, handle):
        self.h = handle
        self.n = 0
    def inc(self, n=1):
        self.n += n
        return self.n


class Prog:
    ENGS = ("sync", "gpsimd", "tensor", "vector", "scalar")
    def __init__(self):
        self.ops = {k: [] for k in self.ENGS}
    def emit(self, eng, fn):
        self.ops[eng].append(fn)


def host_consts():
    g = 1.0 - 2.0 ** (-5.0 - np.arange(NH, dtype=np.float64))
    t = np.arange(P, dtype=np.float64)
    diff = t[None, :] - t[:, None]
    maskD = np.zeros((P, NH * P), np.float64)
    for h in range(NH):
        maskD[:, h * P:(h + 1) * P] = np.where(diff >= 0, g[h] ** np.maximum(diff, 0) / 8.0, 0.0)
    decQ = np.zeros((P, KC * P), np.float64)
    for i in range(KC):
        for hf in range(2):
            hh = 2 * i + hf
            decQ[hf * DH:(hf + 1) * DH, i * P:(i + 1) * P] = (g[hh] ** (t + 1.0) / 8.0)[None, :]
    decK = np.zeros((P, KC * P), np.float64)
    for i in range(KC):
        for hf in range(2):
            hh = 2 * i + hf
            decK[:, i * P + hf * DH: i * P + (hf + 1) * DH] = (g[hh] ** (127.0 - t))[:, None]
    gC = np.zeros((P, 512), np.float64)
    for h in range(NH):
        gC[DH * (h % 2):DH * (h % 2 + 1), (h // 2) * DH:(h // 2 + 1) * DH] = g[h] ** 128.0
    return {
        "maskD": np.ascontiguousarray(maskD.astype(ml_dtypes.bfloat16)),
        "decQ": np.ascontiguousarray(decQ.astype(ml_dtypes.bfloat16)),
        "decK": np.ascontiguousarray(decK.astype(ml_dtypes.bfloat16)),
        "gC": np.ascontiguousarray(gC.astype(np.float32)),
        "gC2": np.ascontiguousarray((gC * gC).astype(np.float32)),
        "idn": np.eye(P, dtype=np.float32),
    }


def build_program():
    nc = bass.Bass("TRN2", target_bir_lowering=False)
    dt = lambda name, shape, dtype=F32, kind="ExternalInput": nc.dram_tensor(name, shape, dtype, kind=kind).ap()

    x_d = dt("x", [NG * Tg, D])
    xp_d = dt("xp", [NPREF * P, D])
    wq_d = dt("Wq", [D, D]); wk_d = dt("Wk", [D, D]); wv_d = dt("Wv", [D, D])
    wg_d = dt("Wg", [D, D]); wo_d = dt("Wo", [D, D])
    bgb_d = dt("bg_b", [P, D], BF16)
    bob_d = dt("bo_b", [P, D], BF16)
    gam_d = dt("gam_b", [P, D], BF16)
    bet_d = dt("bet_b", [P, D], BF16)
    w1_d = dt("W1", [E, D, H]); b1_d = dt("b1", [E, H])
    w2_d = dt("W2", [E, H, D]); b2_d = dt("b2", [E, D])
    wr_d = dt("Wr", [D, E]); brb_d = dt("br_b", [P, E])
    maskD_d = dt("maskD", [P, NH * P], BF16)
    decQ_d = dt("decQ", [P, KC * P], BF16)
    decK_d = dt("decK", [P, KC * P], BF16)
    gC_d = dt("gC", [P, 512])
    gC2_d = dt("gC2", [P, 512])
    idn_d = dt("idn", [P, P])
    y_d = dt("y", [NG * Tg, D], F32, kind="ExternalOutput")
    xbf_scr = dt("xbf_scr", [NG * Tg, D], BF16, kind="Internal")
    xpbf_scr = dt("xpbf_scr", [NPREF * P, D], BF16, kind="Internal")
    wA_scr = dt("wA_scr", [5 * P, KC * D], BF16, kind="Internal")
    w1_scr = dt("w1_scr", [E * P, KC * H], BF16, kind="Internal")
    w2_scr = dt("w2_scr", [E * P, HC * D], BF16, kind="Internal")

    ctx = ExitStack()
    sb = lambda n, s, d=F32: ctx.enter_context(nc.sbuf_tensor(n, s, d))
    ps = lambda n, s, d=F32: ctx.enter_context(nc.psum_tensor(n, s, d))

    acc = sb("acc", [P, TC * D])
    y1fm = sb("y1fm", [P, KC * Tg], BF16)
    regC = sb("regC", [P, HC * D], BF16)
    w2buf = regC
    WQ_OFF, WK_OFF, WV_OFF, WG_OFF = 0, KC * D, 2 * KC * D, 3 * KC * D
    regD = sb("regD", [P, HCH * Tg], BF16)
    hbuf = regD
    WO_OFF = 0
    QF_OFF = KC * D
    KF_OFF = KC * D + KC * 512
    regE = sb("regE", [P, 2 * KC * SLAB_H], BF16)
    w1buf = [regE[:, i * KC * SLAB_H:(i + 1) * KC * SLAB_H] for i in range(2)]
    XF_OFF = 0
    VT_OFF = KC * 512
    GT_OFF = KC * 512 + D
    KT_OFF = KC * 512 + 2 * D
    QD_OFF = KC * 512 + 3 * D
    regF = sb("regF", [P, 3 * D])
    xtm = [regF[:, 0:D], regF[:, D:2 * D]]
    t1r1 = regF[:, 2 * D:3 * D]
    outsb = [regF[:, 0:D], regF[:, D:2 * D]]
    b1stg = regF[0:HC, 0:E * P]
    b2stg = regF[0:E, D:2 * D]
    regG = sb("regG", [P, 3 * D], BF16)
    lnG = regG[:, 0:D]
    stbf = [regG[:, D:D + 512], regG[:, D + 512:2 * D]]
    xy1bfc = regG[:, 2 * D:3 * D]
    wfmhi = regG[0:E, D:2 * D]
    wfmlo = regG[0:E, 2 * D:3 * D]
    maskD = sb("maskDs", [P, NH * P], BF16)
    decQ = sb("decQs", [P, KC * P], BF16)
    decK = sb("decKs", [P, KC * P], BF16)
    gCt = sb("gCt", [P, 512])
    gC2t = sb("gC2t", [P, 512])
    gamb = sb("gamb", [P, D], BF16)
    betb = sb("betb", [P, D], BF16)
    bgb = sb("bgb", [P, D], BF16)
    bob = sb("bob", [P, D], BF16)
    idn_f = sb("idn_f", [P, P])
    idn_b = sb("idn_b", [P, P], BF16)
    b1fm = sb("b1fm", [P, E * HC])
    b2hi = sb("b2hi", [E, D], BF16)
    b2lo = sb("b2lo", [E, D], BF16)
    wrsb = sb("wrsb", [P, KC * E], BF16)
    brb = sb("brb", [P, E])
    S = sb("S", [P, 512])
    S2 = regF[:, 2 * D:2 * D + 512]
    Sbf = sb("Sbf", [P, 512], BF16)
    wsb = sb("wsb", [P, TC * E])
    mvar = sb("mvar", [P, 2])
    bnbuf = sb("bnbuf", [P, 12])
    lnv = sb("lnv", [P, 1])
    rstd = sb("rstd", [P, 1])
    epsi = sb("epsi", [P, 1])
    eps5 = sb("eps5", [P, 1])
    rl = sb("rl", [P, CPT * E])
    rl2 = sb("rl2", [P, CPT * E])
    rm1 = sb("rm1", [P, CPT])
    rm2 = sb("rm2", [P, CPT])
    req = sb("req", [P, CPT * E])
    rge = req
    rexp = rl
    rsum = sb("rsum", [P, CPT])
    rrs = rm1

    pA = ps("pA", [P, D])
    pB = ps("pB", [P, D])
    pC = ps("pC", [P, D], BF16)
    pEF = ps("pEF", [P, 1024])
    pE = pEF[:, 0:512]
    pF = pEF[:, 512:1024]
    PE_QUARTER = pEF[:, 0:256]
    PF_QUARTER = pEF[:, 512:768]
    pW = ps("pW", [P, 512])

    sem_names = ("q_c", "q_x0", "q_x1", "q_xc", "q_xe", "q_wa", "q_w2", "q_w1a", "q_w1b", "q_o0", "q_o1", "q_cast", "pe", "act", "dve")
    sems = {k: Ctr(ctx.enter_context(nc.semaphore(k))) for k in sem_names}
    _WA_NAMES = ("wq", "wk", "wv", "wg", "wo")
    (q_c, q_x0, q_x1, q_xc, q_xe, q_wa, q_w2, q_w1a, q_w1b, q_o0, q_o1, q_cast, pe, act, dve) = (sems[k] for k in sem_names)
    q_x = [q_x0, q_x1]
    q_w1 = [q_w1a, q_w1b]
    q_o = [q_o0, q_o1]
    prog = Prog()

    def vop(fn, n_inc=1):
        wait_n = dve.n
        def f(v):
            v.wait_ge(dve.h, wait_n)
            fn(v).then_inc(dve.h, n_inc)
        prog.emit("vector", f)
        return dve.inc(n_inc)

    def aop(fn, n_inc=1):
        wait_n = act.n
        def f(s):
            s.wait_ge(act.h, wait_n)
            fn(s).then_inc(act.h, n_inc)
        prog.emit("scalar", f)
        return act.inc(n_inc)

    def eng_waits(eng, **kw):
        items = [(sems[k].h, v) for k, v in kw.items() if v > 0]
        if not items:
            return
        def f(e):
            for h, v in items:
                e.wait_ge(h, v)
        prog.emit(eng, f)

    def sync_dma(fn, sem):
        def f(s):
            fn(s).then_inc(sem.h, 16)
        prog.emit("sync", f)
        return sem.inc(16)

    def gp_dma(fn, sem, wait_pe=0):
        def f(g):
            if wait_pe:
                g.wait_ge(pe.h, wait_pe)
            fn(g).then_inc(sem.h, 16)
        prog.emit("gpsimd", f)
        return sem.inc(16)

    st = {
        "xtm_free_dve": [0, 0],
        "xbfc_free_pe": 0,
        "pc_free": {},
        "xfm_ready": {},
        "qk_ready": {},
        "vt_ready": {},
        "gt_ready": {},
        "kt_ready": {},
        "qd_ready": {},
        "sbf_ready": {},
        "pe_pf_free": [{}, {}],
        "pA_free": {},
        "pB_free": {},
        "pW_free": {},
        "y1fm_ready": {},
        "st_free_pe_last": 0,
        "ea_ebuf_dve": 0,
        "prep_xt_pe": 0,
        "prep_q_xe": 0,
        "rl_pe": 0,
        "xbob": {},
        "xd": [0, 0],
        "gG_ready": {},
        "bG_ready": {},
        "lnG_ready": {},
        "w1_free_pe": [0, 0],
        "w2_free_pe": 0,
        "wa": {},
        "wfm_reader_pe": 0,
        "ph_free": [{}, {}, {}],
        "ffn_last_pe": 0,
        "outsb_dma": [0, 0],
        "out_dma": [],
    }
    po_free_dve = [0, 0]

    C_LOADS = sync_dma(lambda s: s.dma_start(out=idn_f[:], in_=idn_d), q_c)
    for dst, src in ((maskD, maskD_d), (decQ, decQ_d), (decK, decK_d), (gCt, gC_d), (gC2t, gC2_d),
                     (gamb, gam_d), (betb, bet_d), (bgb, bgb_d), (bob, bob_d), (brb, brb_d)):
        C_LOADS = sync_dma(lambda s, dst=dst, src=src: s.dma_start(out=dst[:], in_=src), q_c)
    C_LOADS = sync_dma(lambda s: s.dma_start(
        out=b1stg[:].rearrange("hc (e p) -> hc e p", e=E),
        in_=b1_d.rearrange("e (hc p) -> hc e p", p=P)), q_c)
    C_LOADS = sync_dma(lambda s: s.dma_start(out=b2stg[:], in_=b2_d), q_c)

    CAST = {}
    wa_sems = {nm: Ctr(ctx.enter_context(nc.semaphore("q_" + nm))) for nm in _WA_NAMES}
    for _nm, _c in wa_sems.items():
        sems["q_" + _nm] = _c
    def cast_dma(name, out_ap, in_ap):
        prev = q_cast.n
        def f(g):
            g.wait_ge(q_cast.h, prev)
            g.dma_start(out=out_ap, in_=in_ap, max_dma_last_dim=8192).then_inc(q_cast.h, 16)
        prog.emit("gpsimd", f)
        CAST[name] = q_cast.inc(16)

    XP_PIECE = 4 * P
    ATTN_W = (("wv", wv_d, 2), ("wk", wk_d, 1), ("wg", wg_d, 3), ("wq", wq_d, 0), ("wo", wo_d, 4))
    def cast_attn_w(nm, src, slot):
        cast_dma(nm, wA_scr[slot * P:(slot + 1) * P, :].rearrange("p (kc n) -> p kc n", kc=KC),
                 src.rearrange("(kc p) n -> p kc n", p=P))
    n_xp = (NPREF * P + XP_PIECE - 1) // XP_PIECE
    _wa0 = {}
    def load_attn_w1(nm, slot):
        castc = CAST[nm]
        slot_off = {"wq": (WQ_OFF, regC), "wk": (WK_OFF, regC), "wv": (WV_OFF, regC),
                    "wg": (WG_OFF, regC), "wo": (WO_OFF, regD)}
        off, reg = slot_off[nm]
        sem = wa_sems[nm]
        def f(g, off=off, reg=reg, slot=slot, castc=castc, sem=sem):
            g.wait_ge(q_cast.h, castc)
            g.dma_start(out=reg[:, off:off + KC * D],
                        in_=wA_scr[slot * P:(slot + 1) * P, :]).then_inc(sem.h, 16)
        prog.emit("gpsimd", f)
        _wa0[nm] = (nm, sem.inc(16))
    cast_dma("xp0", xpbf_scr[0:min(XP_PIECE, NPREF * P), :], xp_d[0:min(XP_PIECE, NPREF * P), :])
    cast_attn_w(*ATTN_W[0]); load_attn_w1("wv", 2)
    cast_attn_w(*ATTN_W[1]); load_attn_w1("wk", 1)
    for i in range(1, n_xp):
        r0, r1 = i * XP_PIECE, min((i + 1) * XP_PIECE, NPREF * P)
        cast_dma(f"xp{i}", xpbf_scr[r0:r1, :], xp_d[r0:r1, :])
    cast_attn_w(*ATTN_W[2]); load_attn_w1("wg", 3)
    cast_attn_w(*ATTN_W[3]); load_attn_w1("wq", 0)
    cast_attn_w(*ATTN_W[4]); load_attn_w1("wo", 4)
    for i in range((NG * Tg + XP_PIECE - 1) // XP_PIECE):
        r0, r1 = i * XP_PIECE, min((i + 1) * XP_PIECE, NG * Tg)
        cast_dma(f"x{i}", xbf_scr[r0:r1, :], x_d[r0:r1, :])
    def cast_w1(e):
        for hh in range(2):
            cast_dma(f"w1_{e}" if hh == 1 else f"w1_{e}_a",
                     w1_scr[e * P:(e + 1) * P, :].rearrange("p (kc h) -> p kc h", kc=KC)[:, :, hh * 2048:(hh + 1) * 2048],
                     w1_d[e][:, hh * 2048:(hh + 1) * 2048].rearrange("(kc p) h -> p kc h", p=P))
    gp_dma(lambda g: g.dma_start(
        out=wrsb[:].rearrange("p (kc e) -> p kc e", kc=KC),
        in_=wr_d.rearrange("(kc p) e -> p kc e", p=P), max_dma_last_dim=32), q_wa)
    WR_READY = q_wa.n
    def cast_w2(e):
        cast_dma(f"w2_{e}", w2_scr[e * P:(e + 1) * P, :].rearrange("p (hc d) -> p hc d", hc=HC),
                 w2_d[e].rearrange("(hc p) d -> p hc d", p=P))
    cast_w1(0)
    cast_w2(0)


    def w_attn_loads(wait_pe):
        wa = {}
        slot_off = {"wq": (WQ_OFF, regC), "wk": (WK_OFF, regC), "wv": (WV_OFF, regC),
                    "wg": (WG_OFF, regC), "wo": (WO_OFF, regD)}
        for nm, src, slot in ATTN_W:
            off, reg = slot_off[nm]
            castc = CAST[nm]
            sem = wa_sems[nm]
            def f(g, nm=nm, off=off, reg=reg, slot=slot, castc=castc, wait_pe=wait_pe, sem=sem):
                g.wait_ge(q_cast.h, castc)
                if wait_pe:
                    g.wait_ge(pe.h, wait_pe)
                g.dma_start(out=reg[:, off:off + KC * D],
                            in_=wA_scr[slot * P:(slot + 1) * P, :]).then_inc(sem.h, 16)
            prog.emit("gpsimd", f)
            wa[nm] = (nm, sem.inc(16))
        st["wa"] = wa

    st["wa"] = dict(_wa0)

    eng_waits("vector", q_c=C_LOADS)
    vop(lambda v: v.memset(epsi[:], LN_EPS_IN))
    vop(lambda v: v.memset(eps5[:], LN_EPS))
    vop(lambda v: v.memset(S[:], 0.0))
    vop(lambda v: v.memset(S2[:], 0.0))
    SBF0 = vop(lambda v: v.memset(Sbf[:], 0.0))
    st["sbf_ready"] = {"dve": SBF0}
    vop(lambda v: v.tensor_copy(idn_b[:], idn_f[:]))
    vop(lambda v: v.tensor_copy(b2hi[:], b2stg[:]))
    PREP = vop(lambda v: v.tensor_tensor(b2lo[:], b2stg[:], b2hi[:], op=ALU.subtract))

    def f(t, w=PREP):
        t.wait_ge(dve.h, w)
        last = None
        for e in range(E):
            last = t.transpose(pW[:, e * HC:(e + 1) * HC], b1stg[:, e * P:(e + 1) * P], idn_f[0:HC, 0:HC])
        last.then_inc(pe.h, 1)
    prog.emit("tensor", f)
    B1_PE = pe.inc()
    eng_waits("vector", pe=B1_PE)
    B1FM = vop(lambda v: v.tensor_copy(b1fm[:], pW[:, 0:E * HC]))
    st["pW_free"] = {"dve": B1FM}
    st["pc_free"] = {"dve": PREP}
    st["xtm_free_dve"] = [B1FM, B1FM]

    def dmax(*deps):
        out = {}
        for d in deps:
            if not d:
                continue
            if isinstance(d, int):
                d = {"dve": d}
            for k, v in d.items():
                out[k] = max(out.get(k, 0), v)
        return out

    def twd(t, deps):
        for k, v in deps.items():
            if v:
                t.wait_ge(sems[k].h, v)

    def ew(eng, deps):
        items = [(sems[k].h, v) for k, v in deps.items() if v > 0]
        if not items:
            return
        def f(e):
            for h, v in items:
                e.wait_ge(h, v)
        prog.emit(eng, f)

    def x_prep(xsrc_bf, castc, cloc, stage=None, extra_ld=None, slot_pe=0):
        stg = xy1bfc if stage is None else stage
        wpe = st["xbfc_free_pe"] if stage is None else 0
        eld = dict(extra_ld or {})
        qs = q_xc if stage is None else q_xe
        def fldr(s_, xsrc_bf=xsrc_bf, castc=castc, wpe=wpe, prev=qs.n, eld=eld, stg=stg, qs=qs):
            s_.wait_ge(qs.h, prev)
            s_.wait_ge(q_cast.h, castc)
            if wpe:
                s_.wait_ge(pe.h, wpe)
            for k, v in eld.items():
                s_.wait_ge(sems[k].h, v)
            s_.dma_start(out=stg[:, 0:D] if stage is not None else xy1bfc[:], in_=xsrc_bf).then_inc(qs.h, 16)
        prog.emit("sync", fldr)
        XC = qs.inc(16)
        if stage is not None:
            st["prep_q_xe"] = XC
        def f(t, w=XC, wb=dict(st["pc_free"]), stg=stg, stage=stage, qs=qs):
            t.wait_ge(qs.h, w)
            twd(t, wb)
            last = None
            src = stg[:, 0:D] if stage is not None else xy1bfc[:]
            for kc in range(KC):
                last = t.transpose(pC[:, kc * P:(kc + 1) * P], src[:, kc * P:(kc + 1) * P], idn_b[:])
            last.then_inc(pe.h, 1)
        prog.emit("tensor", f)
        XT_PE = pe.inc()
        if stage is None:
            st["xbfc_free_pe"] = XT_PE
        else:
            st["prep_xt_pe"] = XT_PE
        ew("scalar", {"pe": max(XT_PE, slot_pe)})
        ev = aop(lambda s, cloc=cloc: s.copy(
            regE[:, XF_OFF:XF_OFF + KC * 512].rearrange("p (kc t) -> p kc t", kc=KC)[:, :, cloc * P:(cloc + 1) * P],
            pC[:].rearrange("p (kc t) -> p kc t", kc=KC)))
        st["pc_free"] = {"act": ev}
        st["xfm_ready"] = {"act": ev}

    def v_proj(cloc, bank=None):
        bank = pB if bank is None else bank
        key = "pB_free" if bank is pB else "pA_free"
        def f(t, wd=dmax(st["xfm_ready"], st[key]), wq=st["wa"]["wv"], cloc=cloc, bank=bank):
            twd(t, wd)
            t.wait_ge(sems["q_" + wq[0]].h, wq[1]) if isinstance(wq, tuple) else t.wait_ge(q_wa.h, wq)
            last = None
            for nh2 in range(2):
                for kc in range(KC):
                    last = t.matmul(bank[:, nh2 * 512:(nh2 + 1) * 512],
                                    regE[:, XF_OFF + kc * 512 + cloc * P: XF_OFF + kc * 512 + (cloc + 1) * P],
                                    regC[:, WV_OFF + kc * D + nh2 * 512: WV_OFF + kc * D + (nh2 + 1) * 512],
                                    start=(kc == 0), stop=(kc == KC - 1))
            last.then_inc(pe.h, 1)
        prog.emit("tensor", f)
        V_PE = pe.inc()
        ew("scalar", {"pe": V_PE})
        ev = aop(lambda s, bank=bank: s.copy(regE[:, VT_OFF:VT_OFF + D], bank[:]))
        st["vt_ready"] = {"act": ev}
        st[key] = {"act": ev}

    def sdelta_mm(bank=None):
        bank = pB if bank is None else bank
        key = "pB_free" if bank is pB else "pA_free"
        def f(t, wd=dmax(st["kt_ready"], st["vt_ready"], st[key]), bank=bank):
            twd(t, wd)
            last = None
            for h in range(NH):
                pb_ = DH * (h % 2)
                col = (h // 2) * DH + 512 * (h % 2)
                dst = bank[pb_:pb_ + DH, col:col + DH]
                last = t.matmul(dst, regE[:, KT_OFF + h * DH:KT_OFF + (h + 1) * DH],
                                regE[:, VT_OFF + h * DH:VT_OFF + (h + 1) * DH],
                                start=(h < 2), stop=(h >= NH - 2), tile_position=(0, pb_))
            last.then_inc(pe.h, 1)
        prog.emit("tensor", f)
        st["sd_pe"] = pe.inc()

    def s_update(bank=None, Sb=None, gb=None, sbf=True):
        bank = pB if bank is None else bank
        Sb = S if Sb is None else Sb
        gb = gCt if gb is None else gb
        key = "pB_free" if bank is pB else "pA_free"
        eng_waits("vector", pe=st["sd_pe"])
        vop(lambda v, Sb=Sb, gb=gb: v.tensor_tensor(Sb[:], Sb[:], gb[:], op=ALU.mult))
        vop(lambda v, Sb=Sb, bank=bank: v.tensor_tensor(Sb[0:DH, 0:512], Sb[0:DH, 0:512], bank[0:DH, 0:512], op=ALU.add))
        SUP = vop(lambda v, Sb=Sb, bank=bank: v.tensor_tensor(Sb[DH:P, 0:512], Sb[DH:P, 0:512], bank[DH:P, 512:1024], op=ALU.add))
        st[key] = {"dve": SUP}
        if sbf:
            ew("scalar", {"dve": SUP})
            st["sbf_ready"] = {"act": aop(lambda s: s.copy(Sbf[:], S[:]))}
        return SUP

    def sdelta_supdate():
        sdelta_mm()
        s_update()

    def prefix_chunk(pidx):
        par = pidx % 2
        bank = pB if par == 0 else pA
        Sb = S if par == 0 else S2
        gb = gC2t
        if pidx == 0:
            x_prep(xpbf_scr[0:P, :], CAST["xp0"], 0)
        if pidx + 1 < NPREF:
            q = pidx + 1
            x_prep(xpbf_scr[q * P:(q + 1) * P, :], CAST[f"xp{q // 4}"], q % 2)
        v_proj(pidx % 2, bank)
        def f(t, wd=dmax(st["xfm_ready"], st["pe_pf_free"][0], st["pe_pf_free"][1]), wq=st["wa"]["wk"],
              sl=pidx % 2):
            twd(t, wd)
            t.wait_ge(sems["q_" + wq[0]].h, wq[1]) if isinstance(wq, tuple) else t.wait_ge(q_wa.h, wq)
            last = None
            for nh2, bank in ((0, pE), (1, pF)):
                for kc in range(KC):
                    last = t.matmul(bank,
                                    regE[:, XF_OFF + kc * 512 + sl * P:XF_OFF + kc * 512 + (sl + 1) * P],
                                    regC[:, WK_OFF + kc * D + nh2 * 512: WK_OFF + kc * D + (nh2 + 1) * 512],
                                    start=(kc == 0), stop=(kc == KC - 1))
            last.then_inc(pe.h, 1)
        prog.emit("tensor", f)
        K_PE = pe.inc()
        eng_waits("vector", pe=K_PE)
        st["pe_pf_free"][0] = {"dve": vop(lambda v: v.tensor_tensor(
            regE[:, KT_OFF:KT_OFF + 512], pE, decK[:, 0:512], op=ALU.mult))}
        kt = vop(lambda v: v.tensor_tensor(
            regE[:, KT_OFF + 512:KT_OFF + D], pF, decK[:, 512:D], op=ALU.mult))
        st["pe_pf_free"][1] = {"dve": kt}
        st["kt_ready"] = {"dve": kt}
        sdelta_mm(bank)
        s_update(bank, Sb, gb, sbf=False)

    def qk_proj():
        last_ev = 0
        for woff, foff in ((WQ_OFF, QF_OFF), (WK_OFF, KF_OFF)):
            wkey = "wq" if woff == WQ_OFF else "wk"
            for nt in range(KC):
                bidx = nt % 2
                bank = pE if bidx == 0 else pF
                def f(t, woff=woff, nt=nt, bank=bank,
                      wd=dmax(st["pe_pf_free"][bidx], st["xfm_ready"]), wq=st["wa"][wkey]):
                    twd(t, wd)
                    t.wait_ge(sems["q_" + wq[0]].h, wq[1]) if isinstance(wq, tuple) else t.wait_ge(q_wa.h, wq)
                    last = None
                    for kc in range(KC):
                        last = t.matmul(bank,
                                        regC[:, woff + kc * D + nt * P: woff + kc * D + (nt + 1) * P],
                                        regE[:, XF_OFF + kc * 512:XF_OFF + (kc + 1) * 512],
                                        start=(kc == 0), stop=(kc == KC - 1))
                    last.then_inc(pe.h, 1)
                prog.emit("tensor", f)
                QK_PE = pe.inc()
                ew("scalar", {"pe": QK_PE})
                last_ev = aop(lambda s, foff=foff, nt=nt, bank=bank: s.copy(
                    regD[:, foff + nt * 512:foff + (nt + 1) * 512], bank[:]))
                st["pe_pf_free"][bidx] = {"act": last_ev}
        st["qk_ready"] = {"act": last_ev}

    ebuf = stbf[0]
    ebuf_full = regG[:, D:2 * D]

    def front_early(g, tb, cloc):
        tc = tb * CPT + cloc
        xb = tc % 2
        ow = {("q_o0" if xb == 0 else "q_o1"): st["outsb_dma"][xb]}
        eng_waits("sync", dve=st["xtm_free_dve"][xb], **ow)
        st["xd"][xb] = sync_dma(lambda s, g=g, tc=tc, xb=xb: s.dma_start(
            out=xtm[xb][:], in_=x_d[(g * Tg + tc * P):(g * Tg + (tc + 1) * P), :]), q_x[xb])
        eng_waits("vector", **{("q_x0" if xb == 0 else "q_x1"): st["xd"][xb]})
        st["xbob"] = {"dve": vop(lambda v, xb=xb: v.tensor_tensor(
            xtm[xb][:], xtm[xb][:], bob[:], op=ALU.add))}

        v_proj(cloc)

        def f(t, wd=dmax(st["xfm_ready"], st["pB_free"]), cloc=cloc, wq=st["wa"]["wg"]):
            twd(t, wd)
            t.wait_ge(sems["q_" + wq[0]].h, wq[1]) if isinstance(wq, tuple) else t.wait_ge(q_wa.h, wq)
            last = None
            for nh2 in range(2):
                for kc in range(KC):
                    last = t.matmul(pB[:, nh2 * 512:(nh2 + 1) * 512],
                                    regE[:, XF_OFF + kc * 512 + cloc * P: XF_OFF + kc * 512 + (cloc + 1) * P],
                                    regC[:, WG_OFF + kc * D + nh2 * 512: WG_OFF + kc * D + (nh2 + 1) * 512],
                                    start=(kc == 0), stop=(kc == KC - 1))
            last.then_inc(pe.h, 1)
        prog.emit("tensor", f)
        G_PE = pe.inc()
        eng_waits("vector", pe=G_PE)
        vop(lambda v: v.tensor_tensor(pB[:, 0:512], pB[:, 0:512], bgb[:, 0:512], op=ALU.add))
        Z1 = vop(lambda v: v.tensor_tensor(pB[:, 512:1024], pB[:, 512:1024], bgb[:, 512:1024], op=ALU.add))
        eng_waits("scalar", dve=max(Z1, st["ea_ebuf_dve"]))
        SIL = aop(lambda s: s.activation(lnG[:], pB[:], AF.Silu))
        eng_waits("vector", act=SIL)
        GG = vop(lambda v: v.tensor_tensor(regE[:, GT_OFF:GT_OFF + D], lnG[:], gamb[:], op=ALU.mult))
        BG = vop(lambda v: v.tensor_tensor(lnG[:], lnG[:], betb[:], op=ALU.mult))
        st["pB_free"] = {"act": SIL}
        st["gG_ready"] = {"dve": GG}
        st["bG_ready"] = {"dve": BG}

        def f(t, wd=dmax(st["pc_free"], st["qk_ready"]), cloc=cloc):
            twd(t, wd)
            last = None
            for nt in range(KC):
                last = t.transpose(pC[:, nt * P:(nt + 1) * P],
                                   regD[:, KF_OFF + nt * 512 + cloc * P: KF_OFF + nt * 512 + (cloc + 1) * P],
                                   idn_b[:])
            last.then_inc(pe.h, 1)
        prog.emit("tensor", f)
        KT_PE = pe.inc()
        eng_waits("vector", pe=KT_PE)
        kt = vop(lambda v: v.tensor_tensor(
            regE[:, KT_OFF:KT_OFF + D], pC[:], decK[:], op=ALU.mult))
        st["pc_free"] = {"dve": kt}
        st["kt_ready"] = {"dve": kt}

        st["qd_ready"] = {"dve": vop(lambda v, cloc=cloc: v.tensor_tensor(
            regE[:, QD_OFF:QD_OFF + D],
            regD[:, QF_OFF:QF_OFF + KC * 512].rearrange("p (nt t) -> p nt t", nt=KC)[:, :, cloc * P:(cloc + 1) * P],
            decQ[:].rearrange("p (nt t) -> p nt t", nt=KC), op=ALU.mult))}

        sdelta_mm()

    def front_late(g, tb, cloc):
        ret_started = [False, False]
        last_ret_pe = 0
        for b4 in range(4):
            bidx = b4 % 2
            def f(t, b4=b4, wd=dmax(st["pe_pf_free"][0], st["pe_pf_free"][1], st["qk_ready"]), cloc=cloc):
                twd(t, wd)
                last = None
                for j in range(4):
                    h = b4 * 4 + j
                    nt = h // 2
                    pbase = (h % 2) * DH
                    bank = pE if j % 2 == 0 else pF
                    col = (j // 2) * P
                    lhsT = regD[pbase:pbase + DH, KF_OFF + nt * 512 + cloc * P: KF_OFF + nt * 512 + (cloc + 1) * P]
                    rhs = regD[pbase:pbase + DH, QF_OFF + nt * 512 + cloc * P: QF_OFF + nt * 512 + (cloc + 1) * P]
                    last = t.matmul(bank[:, col:col + P], lhsT, rhs, start=True, stop=True)
                last.then_inc(pe.h, 1)
            prog.emit("tensor", f)
            SC_PE = pe.inc()
            eng_waits("vector", pe=SC_PE)
            src = bass.AP(pEF, 0, [[pEF.shape[1], P], [512, 2], [P, 2], [1, P]])
            dst = bass.AP(regG, D + bidx * 512, [[3 * D, P], [P, 2], [2 * P, 2], [1, P]])
            msk = bass.AP(maskD, b4 * 4 * P, [[NH * P, P], [P, 2], [2 * P, 2], [1, P]])
            ST_DVE = vop(lambda v, src=src, dst=dst, msk=msk: v.tensor_tensor(dst, src, msk, op=ALU.mult))
            st["pe_pf_free"][0] = {"dve": ST_DVE}
            st["pe_pf_free"][1] = {"dve": ST_DVE}
            s0, s1 = ret_started
            def f(t, b4=b4, bidx=bidx,
                  wd=dmax({"dve": ST_DVE}, st["vt_ready"], st["qd_ready"], st["sbf_ready"], st["pA_free"]),
                  s0=s0, s1=s1):
                twd(t, wd)
                started = [s0, s1]
                last = None
                for j in range(4):
                    h = b4 * 4 + j
                    bk = 0 if h < 8 else 1
                    dst = pA[:, h * DH:(h + 1) * DH]
                    last = t.matmul(dst, stbf[bidx][:, j * P:(j + 1) * P],
                                    regE[:, VT_OFF + h * DH:VT_OFF + (h + 1) * DH],
                                    start=(not started[bk]), stop=False)
                    started[bk] = True
                    pbase = (h % 2) * DH
                    qd = regE[pbase:pbase + DH, QD_OFF + (h // 2) * P: QD_OFF + (h // 2 + 1) * P]
                    sbf_sl = Sbf[pbase:pbase + DH, (h // 2) * DH:(h // 2 + 1) * DH]
                    last = t.matmul(dst, qd, sbf_sl, start=False, stop=(h in (7, 15)))
                last.then_inc(pe.h, 1)
            prog.emit("tensor", f)
            last_ret_pe = pe.inc()
            if b4 < 2:
                ret_started[0] = True
            else:
                ret_started[1] = True
            if b4 == 1:
                eng_waits("vector", pe=last_ret_pe)
                vop(lambda v: v.bn_stats(bnbuf[:, 0:6], pA[:, 0:512]))
        st["st_free_pe_last"] = last_ret_pe
        eng_waits("vector", pe=last_ret_pe)
        vop(lambda v: v.bn_stats(bnbuf[:, 6:12], pA[:, 512:1024]))
        s_update()

    def epi_a(g, tb, cloc):
        BN = vop(lambda v: v.bn_aggr(mvar[:], bnbuf[:].rearrange("p (n t) -> p n t", t=3)))
        eng_waits("scalar", dve=BN)
        aop(lambda s: s.activation(lnv[:], mvar[:, 1:2], AF.Ln, bias=epsi[:]))
        RS = aop(lambda s: s.activation(rstd[:], lnv[:], AF.Exp, scale=-0.5))
        eng_waits("vector", act=RS, pe=st["prep_xt_pe"], q_xe=st["prep_q_xe"])
        TS1 = vop(lambda v: v.tensor_scalar(ebuf_full, pA[:], mvar[:, 0:1], rstd[:], op0=ALU.subtract, op1=ALU.mult))
        st["pA_free"] = {"dve": TS1}
        vop(lambda v: v.tensor_tensor(ebuf_full, ebuf_full, regE[:, GT_OFF:GT_OFF + D], op=ALU.mult))
        EAE = vop(lambda v: v.tensor_tensor(xy1bfc[:], ebuf_full, lnG[:], op=ALU.add))
        st["lnG_ready"] = {"dve": EAE}
        st["ea_ebuf_dve"] = EAE

    def epi_b(g, tb, cloc):
        tc = tb * CPT + cloc
        xb = tc % 2
        def f(t, wd=dmax(st["lnG_ready"], st["pc_free"], {"pe": st["xbfc_free_pe"]})):
            twd(t, wd)
            last = None
            for kc in range(KC):
                last = t.transpose(pC[:, kc * P:(kc + 1) * P], xy1bfc[:, kc * P:(kc + 1) * P], idn_b[:])
            last.then_inc(pe.h, 1)
        prog.emit("tensor", f)
        LT_PE = pe.inc()
        st["xbfc_free_pe"] = LT_PE
        ew("scalar", {"pe": LT_PE})
        LNGF = aop(lambda s, cloc=cloc: s.copy(
            regE[:, XF_OFF:XF_OFF + KC * 512].rearrange("p (kc t) -> p kc t", kc=KC)[:, :, cloc * P:(cloc + 1) * P],
            pC[:].rearrange("p (kc t) -> p kc t", kc=KC)))
        st["pc_free"] = {"act": LNGF}

        def f(t, wd=dmax({"act": LNGF}, st["pe_pf_free"][0], st["pe_pf_free"][1]), wq=st["wa"]["wo"], cloc=cloc):
            twd(t, wd)
            t.wait_ge(sems["q_" + wq[0]].h, wq[1]) if isinstance(wq, tuple) else t.wait_ge(q_wa.h, wq)
            last = None
            for nh2, bank in ((0, pE), (1, pF)):
                for kc in range(KC):
                    last = t.matmul(bank,
                                    regE[:, XF_OFF + kc * 512 + cloc * P: XF_OFF + kc * 512 + (cloc + 1) * P],
                                    regD[:, WO_OFF + kc * D + nh2 * 512: WO_OFF + kc * D + (nh2 + 1) * 512],
                                    start=(kc == 0), stop=(kc == KC - 1))
            last.then_inc(pe.h, 1)
        prog.emit("tensor", f)
        AO_PE = pe.inc()
        st["ao_pe"] = AO_PE

        eng_waits("vector", pe=AO_PE)
        st["pe_pf_free"][0] = {"dve": vop(lambda v, xb=xb: v.tensor_tensor(
            t1r1[:, 0:512], pE, xtm[xb][:, 0:512], op=ALU.add))}
        R1X = vop(lambda v, xb=xb: v.tensor_tensor(
            t1r1[:, 512:1024], pF, xtm[xb][:, 512:1024], op=ALU.add))
        st["pe_pf_free"][1] = {"dve": R1X}
        st["xtm_free_dve"][xb] = R1X
        vop(lambda v: v.bn_stats(bnbuf[:, 0:6], t1r1[:, 0:512]))
        vop(lambda v: v.bn_stats(bnbuf[:, 6:12], t1r1[:, 512:1024]))
        BN2 = vop(lambda v: v.bn_aggr(mvar[:], bnbuf[:].rearrange("p (n t) -> p n t", t=3)))
        eng_waits("scalar", dve=BN2)
        aop(lambda s: s.activation(lnv[:], mvar[:, 1:2], AF.Ln, bias=eps5[:]))
        RS2 = aop(lambda s: s.activation(rstd[:], lnv[:], AF.Exp, scale=-0.5))
        eng_waits("vector", act=RS2)
        ACC_W = vop(lambda v, tc=tc: v.tensor_scalar(acc[:, tc * D:(tc + 1) * D], t1r1[:],
                                                     mvar[:, 0:1], rstd[:], op0=ALU.subtract, op1=ALU.mult))

        ew("scalar", {"pe": st["xbfc_free_pe"], "dve": ACC_W})
        Y1C = aop(lambda s, tc=tc: s.copy(xy1bfc[:], acc[:, tc * D:(tc + 1) * D]))
        def f(t, wd=dmax({"act": Y1C}, st["pc_free"])):
            twd(t, wd)
            last = None
            for kc in range(KC):
                last = t.transpose(pC[:, kc * P:(kc + 1) * P], xy1bfc[:, kc * P:(kc + 1) * P], idn_b[:])
            last.then_inc(pe.h, 1)
        prog.emit("tensor", f)
        YT_PE = pe.inc()
        st["xbfc_free_pe"] = YT_PE
        ew("scalar", {"pe": YT_PE})
        Y1F = aop(lambda s, tc=tc: s.copy(
            y1fm[:].rearrange("p (kc t) -> p kc t", kc=KC)[:, :, tc * P:(tc + 1) * P],
            pC[:].rearrange("p (kc t) -> p kc t", kc=KC)))
        st["pc_free"] = {"act": Y1F}
        st["y1fm_ready"] = {"act": Y1F}

        def f(t, wd=dmax({"act": Y1F}, st["pW_free"]), wq=WR_READY, tc=tc, cloc=cloc):
            twd(t, wd)
            t.wait_ge(sems["q_" + wq[0]].h, wq[1]) if isinstance(wq, tuple) else t.wait_ge(q_wa.h, wq)
            last = None
            for kc in range(KC):
                last = t.matmul(pW[:, cloc * E:(cloc + 1) * E],
                                y1fm[:].rearrange("p (kc t) -> p kc t", kc=KC)[:, kc, tc * P:(tc + 1) * P],
                                wrsb[:, kc * E:(kc + 1) * E], start=(kc == 0), stop=(kc == KC - 1))
            last.then_inc(pe.h, 1)
        prog.emit("tensor", f)
        st["rl_pe"] = pe.inc()

    def router_tblock(g, tb):
        v3 = lambda ap: ap.rearrange("p (c e) -> p c e", c=CPT)
        bc = lambda t4: bass.AP(t4, 0, [[CPT, P], [1, CPT], [0, E]])
        eng_waits("vector", pe=st["rl_pe"])
        vop(lambda v: v.tensor_tensor(
            rl[:].rearrange("p (c e) -> p c e", c=CPT),
            pW[:, 0:CPT * E].rearrange("p (c e) -> p c e", c=CPT),
            bass.AP(brb, 0, [[E, P], [0, CPT], [1, E]]), op=ALU.add))
        RL_D = vop(lambda v: v.tensor_reduce(rm1[:], v3(rl[:]), op=ALU.max, axis=AX.X))
        st["pW_free"] = {"dve": RL_D}
        vop(lambda v: v.tensor_tensor(req[:].rearrange("p (c e) -> p c e", c=CPT),
                                      v3(rl[:]), bc(rm1), op=ALU.is_equal))
        vop(lambda v: v.scalar_tensor_tensor(rl2[:], req[:], -1e30, rl[:], op0=ALU.mult, op1=ALU.add))
        vop(lambda v: v.tensor_reduce(rm2[:], v3(rl2[:]), op=ALU.max, axis=AX.X))
        vop(lambda v: v.tensor_tensor(rge[:].rearrange("p (c e) -> p c e", c=CPT),
                                      v3(rl[:]), bc(rm2), op=ALU.is_ge))
        SUB = vop(lambda v: v.tensor_tensor(rl2[:].rearrange("p (c e) -> p c e", c=CPT),
                                            v3(rl[:]), bc(rm1), op=ALU.subtract))
        eng_waits("scalar", dve=SUB)
        REXP = aop(lambda s: s.activation(rexp[:], rl2[:], AF.Exp))
        eng_waits("vector", act=REXP)
        vop(lambda v: v.tensor_tensor(rexp[:], rexp[:], rge[:], op=ALU.mult))
        vop(lambda v: v.tensor_reduce(rsum[:], v3(rexp[:]), op=ALU.add, axis=AX.X))
        vop(lambda v: v.reciprocal(rrs[:], rsum[:]))
        WSB = vop(lambda v, tb=tb: v.tensor_tensor(
            wsb[:, tb * CPT * E:(tb + 1) * CPT * E].rearrange("p (c e) -> p c e", c=CPT),
            v3(rexp[:]), bc(rrs), op=ALU.mult))
        return WSB


    w1_ready = {}
    def w1_load(key, e, half, slab, extra_pe=0):
        buf = slab % 2
        wait_pe = max(st["w1_free_pe"][buf], extra_pe)
        sem = q_w1[buf]
        castc = CAST[f"w1_{e}"]
        sg = half * N_SLAB + slab
        def f(s_, e=e, sg=sg, buf=buf, wait_pe=wait_pe, sem=sem, castc=castc):
            s_.wait_ge(q_cast.h, castc)
            if wait_pe:
                s_.wait_ge(pe.h, wait_pe)
            s_.dma_start(out=w1buf[buf].rearrange("p (kc h) -> p kc h", kc=KC),
                         in_=w1_scr[e * P:(e + 1) * P, :].rearrange(
                             "p (kc h) -> p kc h", kc=KC)[:, :, sg * SLAB_H:(sg + 1) * SLAB_H]
                         ).then_inc(sem.h, 16)
        prog.emit("sync", f)
        w1_ready[key] = (buf, sem.inc(16))

    w2_ready = {}
    def w2_load(key, e, wait_pe):
        castc = CAST[f"w2_{e}"]
        def f(g_, e=e, wait_pe=wait_pe, castc=castc):
            g_.wait_ge(q_cast.h, castc)
            if wait_pe:
                g_.wait_ge(pe.h, wait_pe)
            g_.dma_start(out=w2buf[:], in_=w2_scr[e * P:(e + 1) * P, :]).then_inc(q_w2.h, 16)
        prog.emit("gpsimd", f)
        w2_ready[key] = q_w2.inc(16)

    def ffn_group(g, wsb_ready, acc_ready_act, y1fm_ready):
        WF_EV = 0
        for rnd in range(2):
            def f(t, wd=dmax({"dve": max(wsb_ready, WF_EV)}, st["pW_free"], {"pe": st["wfm_reader_pe"]}), rnd=rnd):
                twd(t, wd)
                last = None
                for i in range(4):
                    tc = rnd * 4 + i
                    last = t.transpose(pW[0:E, i * P:(i + 1) * P], wsb[:, tc * E:(tc + 1) * E], idn_f[:])
                last.then_inc(pe.h, 1)
            prog.emit("tensor", f)
            WT_PE = pe.inc()
            eng_waits("vector", pe=WT_PE)
            vop(lambda v, rnd=rnd: v.tensor_copy(wfmhi[0:E, rnd * 512:(rnd + 1) * 512], pW[0:E, 0:512]))
            WF_EV = vop(lambda v, rnd=rnd: v.tensor_tensor(
                wfmlo[0:E, rnd * 512:(rnd + 1) * 512], pW[0:E, 0:512],
                wfmhi[0:E, rnd * 512:(rnd + 1) * 512], op=ALU.subtract))
        st["pW_free"] = {"dve": WF_EV}
        st["ph_free"][2] = dmax(st["ph_free"][2], {"dve": WF_EV})
        WFM_READY = WF_EV

        po_free = [dmax({"dve": po_free_dve[0]}, st["pA_free"]),
                   dmax({"dve": po_free_dve[1]}, st["pB_free"])]
        last_b2_pe = 0
        for tc in range(TC):
            pb = tc % 2
            bank = pA if pb == 0 else pB
            def f(t, tc=tc, bank=bank, wd=dmax({"dve": max(WFM_READY, PREP)}, po_free[pb], st["pA_free"])):
                twd(t, wd)
                last = None
                for nh in range(2):
                    dst = bank[:, nh * 512:(nh + 1) * 512]
                    for i, (lhs, rhs) in enumerate(((wfmhi, b2hi), (wfmhi, b2lo), (wfmlo, b2hi))):
                        last = t.matmul(dst, lhs[0:E, tc * P:(tc + 1) * P],
                                        rhs[0:E, nh * 512:(nh + 1) * 512],
                                        start=(i == 0), stop=(i == 2))
                last.then_inc(pe.h, 1)
            prog.emit("tensor", f)
            last_b2_pe = pe.inc()
            eng_waits("vector", pe=last_b2_pe, act=acc_ready_act)
            po_free_dve[pb] = vop(lambda v, tc=tc, bank=bank: v.tensor_tensor(
                acc[:, tc * D:(tc + 1) * D], acc[:, tc * D:(tc + 1) * D], bank[:], op=ALU.add))
            po_free[pb] = {"dve": po_free_dve[pb]}
        st["wfm_reader_pe"] = last_b2_pe

        pepf0 = [dict(st["pe_pf_free"][0]), dict(st["pe_pf_free"][1])]
        for e in range(E):
            if g == 0 and e + 1 < E:
                cast_w1(e + 1)
                cast_w2(e + 1)
            for half in range(NHALF):
                evac_start = act.n
                gelu_cnt = [0] * HCH
                for hcl in range(HCH):
                    slab = hcl // HC_PER_SLAB
                    key = (g, e, half, slab)
                    if key not in w1_ready:
                        w1_load(key, e, half, slab)
                    for th in range(2):
                        it = hcl * 2 + th
                        b = (half * HCH * 2 + it) % 3
                        bank = (pE, pF, pW[:])[b]
                        wqbuf, wqcnt = w1_ready[key]
                        def f(t, e=e, half=half, hcl=hcl, th=th, b=b, bank=bank,
                              wqbuf=wqbuf, wqcnt=wqcnt,
                              wd=dmax(st["ph_free"][b], y1fm_ready, pepf0[b % 2])):
                            t.wait_ge(q_w1[wqbuf].h, wqcnt)
                            twd(t, wd)
                            last = None
                            hloc = hcl % HC_PER_SLAB
                            wb_ = w1buf[(hcl // HC_PER_SLAB) % 2]
                            for kc in range(KC):
                                lhsT = wb_[:, kc * SLAB_H + hloc * P: kc * SLAB_H + (hloc + 1) * P]
                                rhs = y1fm[:, kc * Tg + th * 512: kc * Tg + (th + 1) * 512]
                                last = t.matmul(bank, lhsT, rhs, start=(kc == 0), stop=(kc == KC - 1))
                            last.then_inc(pe.h, 1)
                        prog.emit("tensor", f)
                        mm_pe = pe.inc()
                        if hcl % HC_PER_SLAB == HC_PER_SLAB - 1 and th == 1:
                            st["w1_free_pe"][(hcl // HC_PER_SLAB) % 2] = mm_pe
                            nslab = slab + 2
                            if nslab < N_SLAB:
                                w1_load((g, e, half, nslab), e, half, nslab)
                            elif half + 1 < NHALF:
                                w1_load((g, e, half + 1, nslab - N_SLAB), e, half + 1, nslab - N_SLAB)
                            elif e + 1 < E:
                                w1_load((g, e + 1, 0, nslab - N_SLAB), e + 1, 0, nslab - N_SLAB)
                        eng_waits("scalar", pe=mm_pe)
                        gc = aop(lambda s, e=e, half=half, hcl=hcl, th=th, bank=bank: s.activation(
                            hbuf[:, hcl * Tg + th * 512: hcl * Tg + (th + 1) * 512],
                            bank, AF.Gelu,
                            bias=b1fm[:, e * HC + half * HCH + hcl: e * HC + half * HCH + hcl + 1]))
                        st["ph_free"][b] = {"act": gc}
                        gelu_cnt[hcl] = gc
                evac_end = act.n
                for tc in range(TC):
                    pb = tc % 2
                    bank = pA if pb == 0 else pB
                    def f(t, e=e, half=half, tc=tc, bank=bank,
                          wact=evac_end, ww2=w2_ready[(g, e)], wd=dict(po_free[pb])):
                        t.wait_ge(act.h, wact)
                        t.wait_ge(q_w2.h, ww2)
                        twd(t, wd)
                        last = None
                        for nh in range(2):
                            dst = bank[:, nh * 512:(nh + 1) * 512]
                            for hcl in range(HCH):
                                hc = half * HCH + hcl
                                lhsT = hbuf[:, hcl * Tg + tc * P: hcl * Tg + (tc + 1) * P]
                                rhs = w2buf[:, hc * D + nh * 512: hc * D + (nh + 1) * 512]
                                last = t.matmul(dst, lhsT, rhs, start=(hcl == 0), stop=(hcl == HCH - 1))
                        last.then_inc(pe.h, 1)
                    prog.emit("tensor", f)
                    mm_pe = pe.inc()
                    if tc == TC - 1 and half == NHALF - 1:
                        st["w2_free_pe"] = mm_pe
                    eng_waits("vector", pe=mm_pe)
                    po_free_dve[pb] = vop(lambda v, e=e, tc=tc, bank=bank: v.scalar_tensor_tensor(
                        acc[:, tc * D:(tc + 1) * D], bank[:], wsb[:, tc * E + e: tc * E + e + 1],
                        acc[:, tc * D:(tc + 1) * D], op0=ALU.mult, op1=ALU.add))
                    po_free[pb] = {"dve": po_free_dve[pb]}
            if e + 1 < E:
                w2_load((g, e + 1), e + 1, st["w2_free_pe"])
        st["ffn_last_pe"] = pe.n
        st["pA_free"] = {"dve": po_free_dve[0]}
        st["pB_free"] = {"dve": po_free_dve[1]}
        st["pW_free"] = dmax(st["pW_free"], st["ph_free"][2])
        st["xbfc_free_pe"] = max(st["xbfc_free_pe"], st["wfm_reader_pe"])

        for tc in range(TC):
            vop(lambda v, tc=tc: v.bn_stats(bnbuf[:, 0:6], acc[:, tc * D: tc * D + 512]))
            vop(lambda v, tc=tc: v.bn_stats(bnbuf[:, 6:12], acc[:, tc * D + 512: tc * D + 1024]))
            BN3 = vop(lambda v: v.bn_aggr(mvar[:], bnbuf[:].rearrange("p (n t) -> p n t", t=3)))
            eng_waits("scalar", dve=BN3)
            aop(lambda s: s.activation(lnv[:], mvar[:, 1:2], AF.Ln, bias=eps5[:]))
            RS3 = aop(lambda s: s.activation(rstd[:], lnv[:], AF.Exp, scale=-0.5))
            eng_waits("vector", act=RS3)
            ob = tc % 2
            prev = st["outsb_dma"][ob]
            eng_waits("vector", **{("q_o0" if ob == 0 else "q_o1"): prev})
            OA = vop(lambda v, tc=tc, ob=ob: v.tensor_scalar(outsb[ob][:], acc[:, tc * D:(tc + 1) * D],
                                                             mvar[:, 0:1], rstd[:], op0=ALU.subtract, op1=ALU.mult))
            def f(s_, g=g, tc=tc, ob=ob, OA=OA):
                s_.wait_ge(dve.h, OA)
                s_.dma_start(out=y_d[g * Tg + tc * P: g * Tg + (tc + 1) * P, :],
                             in_=outsb[ob][:]).then_inc(q_o[ob].h, 16)
            prog.emit("sync", f)
            st["outsb_dma"][ob] = q_o[ob].inc(16)

    for pidx in range(NPREF):
        prefix_chunk(pidx)

    vop(lambda v: v.tensor_tensor(S[:], S[:], gCt[:], op=ALU.mult))
    CMB = vop(lambda v: v.tensor_tensor(S[:], S[:], S2[:], op=ALU.add))
    ew("scalar", {"dve": CMB})
    st["sbf_ready"] = {"act": aop(lambda s: s.copy(Sbf[:], S[:]))}

    if DEBUG_STAGE == "prefix":
        SS = st["sbf_ready"].get("dve", 0)
        SSA = st["sbf_ready"].get("act", 0)
        def fdbg(s_, SS=SS, SSA=SSA):
            if SS:
                s_.wait_ge(dve.h, SS)
            if SSA:
                s_.wait_ge(act.h, SSA)
            s_.dma_start(out=y_d[0:P, 0:512], in_=S[:]).then_inc(q_o0.h, 16)
            s_.wait_ge(q_o0.h, 16)
        prog.emit("sync", fdbg)
        _assemble(nc, prog, ctx)
        return nc

    for g in range(NG):
        wsb_last = 0
        for tb in range(NTB):
            if tb == 0:
                for cloc in range(CPT):
                    tc = cloc
                    gtc = g * TC + tc
                    x_prep(xbf_scr[(g * Tg + tc * P):(g * Tg + (tc + 1) * P), :], CAST[f"x{gtc // 4}"], cloc)
            qk_proj()
            front_early(g, tb, 0)
            for cloc in range(CPT):
                front_late(g, tb, cloc)
                epi_a(g, tb, cloc)
                if cloc + 1 < CPT:
                    front_early(g, tb, cloc + 1)
                epi_b(g, tb, cloc)
                if tb == 0:
                    tc1 = CPT + cloc
                    gtc1 = g * TC + tc1
                    x_prep(xbf_scr[(g * Tg + tc1 * P):(g * Tg + (tc1 + 1) * P), :], CAST[f"x{gtc1 // 4}"],
                           cloc, stage=ebuf_full,
                           extra_ld={"dve": st["ea_ebuf_dve"], "pe": st["st_free_pe_last"]},
                           slot_pe=st["ao_pe"])
            wsb_last = router_tblock(g, tb)
        attn_end_pe = pe.n
        if DEBUG_STAGE == "attn":
            AC = act.n
            WS = dve.n
            for tc in range(TC):
                def fdbg(s_, g=g, tc=tc, AC=AC, WS=WS):
                    s_.wait_ge(act.h, AC)
                    s_.wait_ge(dve.h, WS)
                    s_.dma_start(out=y_d[g * Tg + tc * P: g * Tg + (tc + 1) * P, :],
                                 in_=acc[:, tc * D:(tc + 1) * D]).then_inc(q_o0.h, 16)
                prog.emit("sync", fdbg)
                q_o0.inc(16)
            st["pA_free"] = dmax(st["pA_free"], {"dve": dve.n})
            st["pB_free"] = dmax(st["pB_free"], {"dve": dve.n})
            continue
        w2_load((g, 0), 0, attn_end_pe)
        w1_load((g, 0, 0, 0), 0, 0, 0, extra_pe=attn_end_pe)
        w1_load((g, 0, 0, 1), 0, 0, 1, extra_pe=attn_end_pe)
        ffn_group(g, wsb_last, act.n, dict(st["y1fm_ready"]))
        if g + 1 < NG:
            w_attn_loads(st["ffn_last_pe"])

    def fin(s_):
        s_.wait_ge(q_o0.h, q_o0.n)
        s_.wait_ge(q_o1.h, q_o1.n)
    prog.emit("sync", fin)

    _assemble(nc, prog, ctx)
    return nc


def _assemble(nc, prog, ctx):
    with nc.Block() as block:
        @block.sync
        def _(s_):
            for fn in prog.ops["sync"]:
                fn(s_)
        @block.gpsimd
        def _(g_):
            for fn in prog.ops["gpsimd"]:
                fn(g_)
        @block.tensor
        def _(t_):
            for fn in prog.ops["tensor"]:
                fn(t_)
        @block.vector
        def _(v_):
            for fn in prog.ops["vector"]:
                fn(v_)
        @block.scalar
        def _(s_):
            for fn in prog.ops["scalar"]:
                fn(s_)
    ctx.close()


def kernel(**inputs):
    x = np.ascontiguousarray(np.asarray(inputs["L0_attn_x"], dtype=np.float32))
    consts = host_consts()
    zeros_pref = np.zeros((NPREF * P, D), np.float32)
    base = {
        "Wq": inputs["L0_attn_Wq"], "Wk": inputs["L0_attn_Wk"], "Wv": inputs["L0_attn_Wv"],
        "Wg": inputs["L0_attn_Wg"], "Wo": inputs["L0_attn_Wo"],
        "W1": inputs["L0_ffn_W1"], "b1": inputs["L0_ffn_b1"],
        "W2": inputs["L0_ffn_W2"], "b2": inputs["L0_ffn_b2"],
        "Wr": inputs["L0_ffn_Wr"],
    }
    base = {k: np.ascontiguousarray(np.asarray(v, dtype=np.float32)) for k, v in base.items()}
    bcast = lambda v, n: np.ascontiguousarray(np.broadcast_to(np.asarray(v, np.float32), (P, n)))
    base["bg_b"] = bcast(inputs["L0_attn_bg"], D).astype(ml_dtypes.bfloat16)
    base["bo_b"] = bcast(inputs["L0_attn_bo"], D).astype(ml_dtypes.bfloat16)
    base["gam_b"] = bcast(inputs["L0_attn_ln_gamma"], D).astype(ml_dtypes.bfloat16)
    base["bet_b"] = bcast(inputs["L0_attn_ln_beta"], D).astype(ml_dtypes.bfloat16)
    base["br_b"] = bcast(inputs["L0_ffn_br"], E)
    base.update(consts)

    in_maps = []
    for c in range(8):
        b, half = c // 2, c % 2
        m = dict(base)
        m["x"] = np.ascontiguousarray(x[b, half * 4096:(half + 1) * 4096])
        m["xp"] = np.ascontiguousarray(x[b, 0:4096]) if half == 1 else zeros_pref
        in_maps.append(m)

    nc = build_program()
    res = run_bass_kernel_spmd(nc, in_maps, core_ids=list(range(8)))
    out = np.empty((4, 8192, 1024), np.float32)
    for c in range(8):
        b, half = c // 2, c % 2
        out[b, half * 4096:(half + 1) * 4096] = res.results[c]["y"]
    return out
